# version 22
# baseline (speedup 1.0000x reference)
import numpy as np
import ml_dtypes

import concourse.bass as bass
import concourse.mybir as mybir
from concourse.bass_utils import run_bass_kernel_spmd

F32 = mybir.dt.float32
BF16 = mybir.dt.bfloat16
AF = mybir.ActivationFunctionType
ALU = mybir.AluOpType
AX = mybir.AxisListType
BF = ml_dtypes.bfloat16

C = 1024
NH = 8
DH = 128
FF = 4096
MASKV = 1280.0
SCALE = 1.0 / 32.0
THS = (0.24419428096993973, 0.4639691338428854, 0.6861859295255306,
       0.9059607823984763, 1.0839784132255623)

ENGS = ("pe", "ve", "ae", "sp", "gp")


def _build_masks():
    mA = np.full((128, 132), -MASKV, np.float32)
    mB = np.full((128, 132), -MASKV, np.float32)
    for p in range(128):
        blk = 4 * (p // 4)
        mA[p, blk + 2: blk + 6] = 0.0
        jB = blk if p % 4 < 2 else blk + 4
        mB[p, jB: jB + 4] = 0.0
    return mA.astype(BF), mB.astype(BF)


class Walk:

    def __init__(self):
        self.prog = {e: [] for e in ENGS}
        self.cnt = {}
        self.wmarks = {}
        self.rmarks = {}
        self.pending = {e: [] for e in ENGS}
        self.barrier_waits = {}
        self.stream_eng = {}

    def barrier(self, skip_streams=()):
        vals = {}
        for e in ENGS:
            if self.prog[e]:
                vals[e] = self._mark_val(e)
        for s, eng in self.stream_eng.items():
            if s not in skip_streams:
                vals[s] = self.cnt.get(s, 0)
        for e in ENGS:
            self.barrier_waits[e] = dict(vals)

    def _mark_val(self, eng):
        if self.pending[eng] or (self.prog[eng] and self.prog[eng][-1]["inc"] is None):
            return self._flush(eng)
        return self.cnt.get(eng, 0)


class Walk2(Walk):
    def op(self, eng, fn, reads=(), writes=(), dma=None, ckpt=False, tag=""):
        waits = {}
        for b in reads:
            for e2 in ENGS:
                if b in self.pending[e2]:
                    self._flush(e2)
            for s, v in self.wmarks.get(b, {}).items():
                waits[s] = max(waits.get(s, 0), v)
        for b in writes:
            for e2 in ENGS:
                if b in self.pending[e2] and e2 != eng:
                    self._flush(e2)
            for s, v in self.wmarks.get(b, {}).items():
                if s != eng:
                    waits[s] = max(waits.get(s, 0), v)
            rm = self.rmarks.get(b, {})
            for s in list(rm):
                v = rm[s]
                if v is None:
                    v = self._flush(s)
                    rm[s] = v
                if s != eng:
                    waits[s] = max(waits.get(s, 0), v)
        if dma is not None:
            self.stream_eng[dma] = eng
            if ckpt:
                prev = self.cnt.get(dma, 0)
                if prev:
                    waits[dma] = max(waits.get(dma, 0), prev)
            v = self.cnt.get(dma, 0) + 16
            self.cnt[dma] = v
            inc = (dma, 16)
            for b in writes:
                self.wmarks[b] = {dma: v}
                self.rmarks[b] = {}
            for b in reads:
                self.rmarks.setdefault(b, {})[dma] = v
        else:
            inc = None
            for b in writes:
                self.pending[eng].append(b)
                self.wmarks[b] = {}
                self.rmarks[b] = {}
            for b in reads:
                self.rmarks.setdefault(b, {})[eng] = None
        bw = self.barrier_waits.pop(eng, None)
        if bw:
            for s, v in bw.items():
                waits[s] = max(waits.get(s, 0), v)
        self.prog[eng].append(dict(fn=fn, waits=waits, inc=inc, tag=tag))
        return len(self.prog[eng]) - 1

    def _flush(self, eng):
        ops = self.prog[eng]
        if not ops:
            return self.cnt.get(eng, 0)
        rec = ops[-1]
        if rec["inc"] is None and rec.get("_dma", False) is False:
            v = self.cnt.get(eng, 0) + 1
            self.cnt[eng] = v
            rec["inc"] = (eng, 1)
            for b in self.pending[eng]:
                self.wmarks.setdefault(b, {})[eng] = v
            self.pending[eng] = []
            for b, rm in self.rmarks.items():
                if rm.get(eng, 0) is None:
                    rm[eng] = v
            return v
        if self.pending[eng] or any(rm.get(eng, 0) is None
                                    for rm in self.rmarks.values()):
            if rec["inc"] is not None and rec["inc"][0] != eng:
                extra = rec.get("inc2")
                if extra is None:
                    v = self.cnt.get(eng, 0) + 1
                    self.cnt[eng] = v
                    rec["inc2"] = (eng, 1)
                for b in self.pending[eng]:
                    self.wmarks.setdefault(b, {})[eng] = self.cnt[eng]
                self.pending[eng] = []
                for b, rm in self.rmarks.items():
                    if rm.get(eng, 0) is None:
                        rm[eng] = self.cnt[eng]
                return self.cnt[eng]
        return self.cnt.get(eng, 0)


def build_program(L=4096):
    NT = L // 128
    NS = L // 512
    nc = bass.Bass("TRN2", target_bir_lowering=False)

    x_d = nc.dram_tensor("x", [L, C], F32, kind="ExternalInput").ap()
    wqkv_d = nc.dram_tensor("w_qkv", [C, 3 * C], F32, kind="ExternalInput").ap()
    wout_d = nc.dram_tensor("w_out", [C, C], F32, kind="ExternalInput").ap()
    bout_d = nc.dram_tensor("b_out", [1, C], F32, kind="ExternalInput").ap()
    wfcT_d = nc.dram_tensor("w_fcT", [C, FF], F32, kind="ExternalInput").ap()
    wprojT_d = nc.dram_tensor("w_projT", [FF, C], F32, kind="ExternalInput").ap()
    out_d = nc.dram_tensor("out", [L, C], F32, kind="ExternalOutput").ap()

    xnT_sc = nc.dram_tensor("xnT_sc", [C, L], BF16, kind="Internal").ap()
    st_sc = nc.dram_tensor("st_sc", [L, C], BF16, kind="Internal").ap()

    mA_np, mB_np = _build_masks()
    eye_h = nc.inline_tensor(np.eye(128, dtype=BF), "c_eye")
    ones_h = nc.inline_tensor(np.ones((1, 128), np.float32), "c_ones")
    mA_h = nc.inline_tensor(mA_np, "c_mA")
    mB_h = nc.inline_tensor(mB_np, "c_mB")
    rz0 = np.ones((128, 1), np.float32); rz0[0:2] = 0.0
    rzL = np.ones((128, 1), np.float32); rzL[126:128] = 0.0
    rf0 = np.zeros((128, 1), np.float32); rf0[0:2] = -0.5
    rfL = np.zeros((128, 1), np.float32); rfL[126:128] = -0.5
    zer_h = nc.inline_tensor(np.zeros((128, 1), np.float32), "c_zero")
    eps_h = nc.inline_tensor(np.full((128, 1), 1e-5, np.float32), "c_eps")
    rz0_h = nc.inline_tensor(rz0, "c_rz0")
    rzL_h = nc.inline_tensor(rzL, "c_rzL")
    rf0_h = nc.inline_tensor(rf0, "c_rf0")
    rfL_h = nc.inline_tensor(rfL, "c_rfL")

    A = {}
    SB_BASE = 16448
    def sb(name, shape, dtype, off):
        A[name] = nc.alloc_sbuf_tensor_at(name, shape, dtype, offset=SB_BASE + off)
        return A[name]

    sb("eye", [128, 128], BF16, 0)
    sb("ones1", [1, 128], F32, 256)
    sb("mA", [128, 132], BF16, 768)
    sb("mB", [128, 132], BF16, 1056)
    sb("rz0", [128, 1], F32, 1344)
    sb("rzL", [128, 1], F32, 1376)
    sb("rf0", [128, 1], F32, 1408)
    sb("rfL", [128, 1], F32, 1440)
    sb("zero", [128, 1], F32, 1472)
    sb("eps", [128, 1], F32, 1504)
    sb("bout", [1, C], F32, 1536)
    sb("b2", [128, C], F32, 5632)
    nc.const_aps.register(F32, 0.0, A["zero"].ap())
    WQ_OFF = 9728
    sb("wq", [128, 8 * 3 * C], BF16, WQ_OFF)
    sb("wo", [128, 8 * C], BF16, WQ_OFF + 49152)
    sb("wfcT", [128, 8 * FF], BF16, WQ_OFF)
    XNT_OFF = WQ_OFF + 65536
    sb("xnT", [128, 8 * L], BF16, XNT_OFF)
    sb("wprT", [128, 32 * C], BF16, XNT_OFF)
    AR = XNT_OFF + 16 * L
    sb("qr", [128, 2 * NH * 512], BF16, AR)
    sb("kr", [128, 2 * NH * 512], BF16, AR + 16384)
    sb("vr", [128, 5 * C], BF16, AR + 32768)
    sb("xt", [128, 2 * C], F32, AR + 43008)
    sb("vx32", [2, C], BF16, AR + 43008)
    sb("kr8", [128, 16], BF16, AR + 45056)
    sb("xnb", [128, 2 * C], BF16, AR + 51200)
    sb("expA", [128, 528], BF16, AR + 55296)
    sb("expB", [128, 528], BF16, AR + 56352)
    sb("wb", [128, 4 * 528], BF16, AR + 57408)
    sb("stt", [128, 2 * C], BF16, AR + 61632)
    sb("wtm", [128, 1024], BF16, AR + 65728)
    sb("osb", [128, 1024], BF16, AR + 67776)
    sb("wte", [4, 1024], BF16, AR + 69824)
    sb("astat", [128, 16], F32, AR + 71872)
    sb("rsb", [128, 32], F32, AR + 71936)
    assert SB_BASE + AR + 72064 <= 229368, f"sbuf overflow B: {SB_BASE + AR + 72064}"
    sb("xnTc", [128, 4096], BF16, AR)
    sb("hsb", [128, 8192], BF16, AR + 8192)
    sb("hrl", [128, 512], BF16, AR + 24576)
    sb("acc", [128, 2048], F32, AR + 25600)
    sb("gsb", [128, 2048], BF16, AR + 33792)
    sb("sti", [128, 4096], BF16, AR + 37888)
    sb("xi", [128, 4096], F32, AR + 46080)
    sb("ot", [128, 2048], F32, AR + 62464)
    assert SB_BASE + AR + 70656 <= 229376

    pp0 = nc.alloc_psum_tensor("pp0", [128, 512], F32)
    pp1 = nc.alloc_psum_tensor("pp1", [128, 512], F32)
    pS0 = nc.alloc_psum_tensor("pS0", [128, 512], F32)
    pS1 = nc.alloc_psum_tensor("pS1", [128, 512], F32)
    pS2 = nc.alloc_psum_tensor("pS2", [128, 512], F32)
    pT = nc.alloc_psum_tensor("pT", [128, 1024], BF16)
    pO = nc.alloc_psum_tensor("pO", [128, 1024], F32)

    w = Walk2()
    rotc = [0]

    def rot():
        k = rotc[0] & 1
        rotc[0] += 1
        return (pp0, "pp0") if k == 0 else (pp1, "pp1")

    def dma(eng, dst, src, stream, reads, writes, ckpt=False, transpose=False):
        if transpose:
            w.op(eng, lambda E, d=dst, s=src: E.dma_start_transpose(d, s),
                 reads=reads, writes=writes, dma=stream, ckpt=ckpt)
        else:
            w.op(eng, lambda E, d=dst, s=src: E.dma_start(d, s),
                 reads=reads, writes=writes, dma=stream, ckpt=ckpt)

    def flags_for(recs):
        first = {}
        last = {}
        for i, (bank, _fn, _r, _w) in enumerate(recs):
            if bank not in first:
                first[bank] = i
            last[bank] = i
        for i, (bank, fn, rd, wr) in enumerate(recs):
            st = (i == first[bank])
            sp_ = (i == last[bank])
            w.op("pe", (lambda E, f=fn, a=st, b=sp_: f(E, a, b)),
                 reads=rd, writes=wr)

    for name, hdl in (("eye", eye_h), ("ones1", ones_h), ("mA", mA_h),
                      ("mB", mB_h), ("rz0", rz0_h), ("rzL", rzL_h),
                      ("rf0", rf0_h), ("rfL", rfL_h), ("zero", zer_h),
                      ("eps", eps_h)):
        dma("sp", A[name][:], hdl.ap(), "smc", (), ("consts", "bout"))
    dma("sp", A["bout"][:], bout_d, "smc", (), ("consts", "bout"))
    for _e in ENGS:
        w.barrier_waits[_e] = {"smc": w.cnt["smc"]}
    for kc in range(8):
        dma("gp", A["wq"][:, 3 * C * kc: 3 * C * (kc + 1)],
            wqkv_d[128 * kc:128 * (kc + 1), :], "swq", (), ("wq",))
    for kc in range(8):
        dma("gp", A["wo"][:, C * kc: C * (kc + 1)],
            wout_d[128 * kc:128 * (kc + 1), :], "swo", (), ("wo",))
    w.op("pe", lambda E: E.matmul(pp0[:], A["ones1"][:], A["bout"][:, 0:512],
                                  start=True, stop=True),
         reads=("bout", "consts"), writes=("pp0",))
    w.op("pe", lambda E: E.matmul(pp1[:], A["ones1"][:], A["bout"][:, 512:1024],
                                  start=True, stop=True),
         reads=("bout", "consts"), writes=("pp1",))
    w.op("ae", lambda E: E.activation(A["b2"][:, 0:512], pp0[:], AF.Copy, scale=2.0),
         reads=("pp0",), writes=("b2",))
    w.op("ae", lambda E: E.activation(A["b2"][:, 512:1024], pp1[:], AF.Copy, scale=2.0),
         reads=("pp1",), writes=("b2",))
    w.op("ae", lambda E: E.memzero(
        A["kr"][:, 0: NH * 512].rearrange("p (h c) -> p h c", h=NH)[:, :, 0:2]),
         writes=("k0",))
    if False:
        w.op("ae", lambda E: E.add_instruction(mybir.InstLoadActFuncSet(
            name=E.bass.get_next_instruction_name(), act_func_set_id=6,
            ins=[], outs=[])))

    inv_c = 1.0 / C

    def ln_load(t):
        p = t % 2
        dma("sp", A["xt"][:, C * p: C * (p + 1)], x_d[128 * t:128 * (t + 1), :],
            "sx" + str(p), (), (f"xt{p}",), ckpt=True)

    def ln_stats(t):
        p = t % 2
        xt = A["xt"][:, C * p: C * (p + 1)]
        xn = A["xnb"][:, C * p: C * (p + 1)]
        st = A["astat"][:, 8 * p: 8 * p + 8]
        w.op("ve", lambda E, o=st, i=xt: E.tensor_reduce(o[:, 0:1], i, AX.X, ALU.add),
             reads=(f"xt{p}",), writes=(f"astat{p}",))
        w.op("ae", lambda E, i=xt, s=st, d=xn: E.activation(d, i, AF.Square,
                                                            accum_out=s[:, 1:2]),
             reads=(f"xt{p}",), writes=(f"astat{p}", f"xnb{p}"))
        w.op("ve", lambda E, s=st: E.tensor_scalar(s[:, 2:3], s[:, 0:1], inv_c, None, ALU.mult),
             reads=(f"astat{p}",), writes=(f"astat{p}",))
        w.op("ve", lambda E, s=st: E.tensor_tensor(s[:, 3:4], s[:, 2:3], s[:, 2:3], ALU.mult),
             reads=(f"astat{p}",), writes=(f"astat{p}",))
        w.op("ve", lambda E, s=st: E.tensor_scalar(s[:, 3:4], s[:, 3:4], -1e-5, None, ALU.add),
             reads=(f"astat{p}",), writes=(f"astat{p}",))
        w.op("ve", lambda E, s=st: E.scalar_tensor_tensor(
            s[:, 4:5], s[:, 1:2], inv_c, s[:, 3:4], ALU.mult, ALU.subtract),
             reads=(f"astat{p}",), writes=(f"astat{p}",))
        w.op("ae", lambda E, s=st: E.activation(s[:, 5:6], s[:, 4:5], AF.Ln),
             reads=(f"astat{p}",), writes=(f"astat{p}",))
        w.op("ae", lambda E, s=st: E.activation(s[:, 6:7], s[:, 5:6], AF.Exp,
                                                scale=-0.5),
             reads=(f"astat{p}",), writes=(f"astat{p}",))
        w.op("ve", lambda E, s=st: E.scalar_tensor_tensor(
            s[:, 7:8], s[:, 2:3], -1.0, s[:, 6:7], ALU.mult, ALU.mult),
             reads=(f"astat{p}",), writes=(f"astat{p}",))

    def ln_apply(t):
        p = t % 2
        xt = A["xt"][:, C * p: C * (p + 1)]
        xn = A["xnb"][:, C * p: C * (p + 1)]
        st = A["astat"][:, 8 * p: 8 * p + 8]
        w.op("ae", lambda E, i=xt, s=st, d=xn: E.activation(
            d, i, AF.Identity, scale=s[:, 6:7], bias=s[:, 7:8]),
             reads=(f"xt{p}", f"astat{p}"), writes=(f"xnb{p}",))

    def xn_transpose(t):
        p4 = t % 2
        bkey = "pT"
        bb = pT[:]
        for c in range(8):
            w.op("pe", lambda E, d=bb[:, 128 * c: 128 * (c + 1)],
                 i=A["xnb"][:, C * p4 + 128 * c: C * p4 + 128 * (c + 1)]:
                 E.transpose(d, i, A["eye"][:]),
                 reads=(f"xnb{p4}", "consts"), writes=(bkey,))
        dst = A["xnT"][:].rearrange("p (c l) -> p c l", c=8)[
            :, :, 128 * t: 128 * (t + 1)]
        src = bb.rearrange("p (c l) -> p c l", c=8)
        if t % 2 == 0:
            w.op("ve", lambda E, d=dst, s=src: E.tensor_copy(d, s),
                 reads=(bkey,), writes=(f"xnTs{t // 4}", "xnT"))
        else:
            w.op("ae", lambda E, d=dst, s=src: E.activation(d, s, AF.Copy),
                 reads=(bkey,), writes=(f"xnTs{t // 4}", "xnT"))

    def qk_super(s, part):
        isq = part < 2
        hs = 4 * (part % 2)
        sl = s % 2
        if isq:
            tok0, N, doff = 512 * s, 512, 0
        elif s > 0:
            tok0, N, doff = 512 * s - 2, 512, 0
        else:
            tok0, N, doff = 0, 510, 2
        qkoff = 0 if isq else C
        dst_sb = A["qr"] if isq else A["kr"]
        key = (f"q{sl}" if isq else f"k{sl}")
        rds = ("wq", f"xnTs{s}") + ((f"xnTs{s - 1}",) if (not isq and s > 0) else ())
        for h in range(hs, hs + 4):
            bank, bkey = rot()
            for kc in range(8):
                w.op("pe", (lambda E, d=bank[:, doff: doff + N],
                            lt=A["wq"][:, 3 * C * kc + qkoff + 128 * h:
                                       3 * C * kc + qkoff + 128 * (h + 1)],
                            r=A["xnT"][:, L * kc + tok0: L * kc + tok0 + N],
                            st_=(kc == 0), sp_=(kc == 7):
                            E.matmul(d, lt, r, start=st_, stop=sp_)),
                     reads=rds, writes=(bkey,))
            w.op("ae", lambda E, d=dst_sb[:, 4096 * sl + 512 * h + doff:
                                          4096 * sl + 512 * h + doff + N],
                 s_=bank[:, doff: doff + N]: E.activation(d, s_, AF.Copy),
                 reads=(bkey,), writes=(key,))

    def v_tile(u):
        vstart = 0 if u == 0 else 128 * u - 2
        sup = vstart // 512
        rds = ("wq", f"xnTs{u // 4}") + ((f"xnTs{sup}",) if sup != u // 4 else ())
        for nh in range(2):
            bank, bkey = rot()
            for kc in range(8):
                w.op("pe", (lambda E, d=bank[:],
                            lt=A["xnT"][:, L * kc + vstart: L * kc + vstart + 128],
                            r=A["wq"][:, 3 * C * kc + 2 * C + 512 * nh:
                                      3 * C * kc + 2 * C + 512 * (nh + 1)],
                            st_=(kc == 0), sp_=(kc == 7):
                            E.matmul(d, lt, r, start=st_, stop=sp_)),
                     reads=rds, writes=(bkey,))
            w.op("ae", lambda E, d=A["vr"][:, C * (u % 5) + 512 * nh:
                                           C * (u % 5) + 512 * (nh + 1)],
                 s_=bank[:]: E.activation(d, s_, AF.Copy),
                 reads=(bkey,), writes=(f"v{u % 5}",))

    def kr8_prod():
        bank, bkey = rot()
        recs = []
        for h in range(8):
            for kc in range(8):
                recs.append((bkey, (lambda E, st_, sp_,
                                    d=bank[:, 2 * h: 2 * h + 2],
                                    lt=A["wq"][:, 3 * C * kc + C + 128 * h:
                                               3 * C * kc + C + 128 * (h + 1)],
                                    r=A["xnT"][:, L * kc + L - 2: L * kc + L]:
                                    E.matmul(d, lt, r, start=st_, stop=sp_)),
                             ("wq", f"xnTs{NS - 1}"), (bkey,)))
        flags_for(recs)
        w.op("ve", lambda E, d=A["kr8"][:], s_=bank[:, 0:16]: E.tensor_copy(d, s_),
             reads=(bkey,), writes=("kr8",))

    def vx32_prod():
        for nh in range(2):
            bank, bkey = rot()
            for kc in range(8):
                w.op("pe", (lambda E, d=bank[0:2, :],
                            lt=A["xnT"][:, L * kc + L - 2: L * kc + L],
                            r=A["wq"][:, 3 * C * kc + 2 * C + 512 * nh:
                                      3 * C * kc + 2 * C + 512 * (nh + 1)],
                            st_=(kc == 0), sp_=(kc == 7):
                            E.matmul(d, lt, r, start=st_, stop=sp_)),
                     reads=("wq", f"xnTs{NS - 1}"), writes=(bkey,))
            w.op("ve", lambda E, d=A["vx32"][0:2, 512 * nh: 512 * (nh + 1)],
                 s_=bank[0:2, :]: E.tensor_copy(d, s_),
                 reads=(bkey,), writes=("vx32",))

    SBANKS = ((pS0, "bkS0"), (pS1, "bkS1"), (pS2, "bkS2"))

    def scores(t, g):
        s_sup, j = t // 4, t % 4
        ksl = (s_sup % 2)
        recs = []
        for hh in range(4):
            h = 4 * g + hh
            lhsT = A["qr"][:, 4096 * ksl + 512 * h + 128 * j:
                           4096 * ksl + 512 * h + 128 * j + 128]
            for seti, mkey in ((0, "mA"), (1, "mB")):
                r = 4 * seti + hh
                ps, bk = SBANKS[r // 3]
                reg0 = 132 * (r % 3)
                segs = []
                if j < 3:
                    segs.append((A["kr"][:, 4096 * ksl + 512 * h + 128 * j:
                                         4096 * ksl + 512 * h + 128 * j + 132],
                                 0, 132, f"k{ksl}"))
                else:
                    segs.append((A["kr"][:, 4096 * ksl + 512 * h + 384:
                                         4096 * ksl + 512 * h + 512],
                                 0, 128, f"k{ksl}"))
                    if t < NT - 1:
                        segs.append((A["kr"][:, 4096 * ((s_sup + 1) % 2) + 512 * h:
                                             4096 * ((s_sup + 1) % 2) + 512 * h + 4],
                                     128, 132, f"k{(s_sup + 1) % 2}"))
                    else:
                        segs.append((A["kr8"][:, 2 * h: 2 * h + 2],
                                     128, 130, "kr8"))
                recs.append((bk, (lambda E, st_, sp_, d=ps[:, reg0: reg0 + 132],
                                  r=A[mkey][:]:
                                  E.matmul(d, A["eye"][:], r, start=st_, stop=sp_)),
                             ("consts",), (bk,)))
                for rhs, c0, c1, kkey in segs:
                    dst = ps[:, reg0 + c0: reg0 + c1]
                    recs.append((bk, (lambda E, st_, sp_, d=dst, lt=lhsT, r=rhs:
                                      E.matmul(d, lt, r, start=st_, stop=sp_)),
                                 (f"q{ksl}", kkey), (bk,)))
        flags_for(recs)

    def softmax_merge(t, g):
        s = g
        ss = str(s)
        s2 = 2 * (t % 2) + g
        expA = A["expA"][:, 0:528]
        expB = A["expB"][:, 0:528]
        rs = A["rsb"]
        wkey = f"wb{t % 2}{g}"
        w.op("ae", lambda E, o=expA[:, 0:396], p_=pS0:
             E.activation(o, p_[:, 0:396], AF.Exp, scale=SCALE),
             reads=("bkS0",), writes=("expA",))
        w.op("ae", lambda E, o=expA[:, 396:528], p_=pS1:
             E.activation(o, p_[:, 0:132], AF.Exp, scale=SCALE),
             reads=("bkS1",), writes=("expA",))
        w.op("ae", lambda E, o=expB[:, 0:264], p_=pS1:
             E.activation(o, p_[:, 132:396], AF.Exp, scale=SCALE),
             reads=("bkS1",), writes=("expB",))
        w.op("ae", lambda E, o=expB[:, 264:528], p_=pS2:
             E.activation(o, p_[:, 0:264], AF.Exp, scale=SCALE),
             reads=("bkS2",), writes=("expB",))
        w.op("ve", lambda E, i=expA, o=rs[:, 16 * s: 16 * s + 4]:
             E.tensor_reduce(o, i.rearrange("p (h j) -> p h j", h=4), AX.X, ALU.add),
             reads=("expA",), writes=("rs" + ss,))
        w.op("ve", lambda E, i=expB, o=rs[:, 16 * s + 4: 16 * s + 8]:
             E.tensor_reduce(o, i.rearrange("p (h j) -> p h j", h=4), AX.X, ALU.add),
             reads=("expB",), writes=("rs" + ss,))
        w.op("ve", lambda E, o=rs[:, 16 * s + 8: 16 * s + 16],
             i=rs[:, 16 * s: 16 * s + 8]: E.reciprocal(o, i),
             reads=("rs" + ss,), writes=("rs" + ss,))
        if t == 0 or t == NT - 1:
            rzk = "rz0" if t == 0 else "rzL"
            w.op("ve", lambda E, o=rs[:, 16 * s + 12: 16 * s + 16], z=A[rzk][:, 0:1]:
                 E.tensor_scalar(o, o, z, None, ALU.mult),
                 reads=("rs" + ss, "consts"), writes=("rs" + ss,))
        for hh in range(4):
            wdst = A["wb"][:, 528 * s2 + 132 * hh: 528 * s2 + 132 * (hh + 1)]
            w.op("ve", lambda E, o=wdst, i=expA[:, 132 * hh: 132 * (hh + 1)],
                 r=rs[:, 16 * s + 8 + hh: 16 * s + 9 + hh]:
                 E.tensor_scalar(o, i, r, None, ALU.mult),
                 reads=("expA", "rs" + ss), writes=(wkey,))
            w.op("ve", lambda E, o=wdst, i=expB[:, 132 * hh: 132 * (hh + 1)],
                 r=rs[:, 16 * s + 12 + hh: 16 * s + 13 + hh]:
                 E.scalar_tensor_tensor(o, i, r, o, ALU.mult, ALU.add),
                 reads=("expB", "rs" + ss, wkey), writes=(wkey,))

    def wt_and_o(t, g):
        s = g
        s2 = 2 * (t % 2) + g
        ss = f"wb{t % 2}{g}"
        shift = 2 if t == 0 else 0
        bkey = "pT"
        bb = pT[:]
        for hh in range(4):
            w.op("pe", lambda E, d=bb[:, 128 * hh: 128 * (hh + 1)],
                 i=A["wb"][:, 528 * s2 + 132 * hh + shift: 528 * s2 + 132 * hh + shift + 128]:
                 E.transpose(d, i, A["eye"][:]),
                 reads=(ss, "consts"), writes=(bkey,))
        for hh in range(4):
            w.op("pe", lambda E, d=bb[0:4, 512 + 128 * hh: 512 + 128 * (hh + 1)],
                 i=A["wb"][:, 528 * s2 + 132 * hh + 128: 528 * s2 + 132 * (hh + 1)]:
                 E.transpose(d, i, A["eye"][:]),
                 reads=(ss, "consts"), writes=(bkey,))
        w.op("ve", lambda E, d=A["wtm"][:, 512 * g: 512 * (g + 1)], s_=bb[:, 0:512]:
             E.tensor_copy(d, s_),
             reads=(bkey,), writes=("wtm",))
        w.op("ve", lambda E, d=A["wte"][0:4, 512 * g: 512 * (g + 1)],
             s_=bb[0:4, 512:1024]: E.tensor_copy(d, s_),
             reads=(bkey,), writes=("wte",))
        if t == 0:
            w.op("ae", lambda E: E.memzero(A["wte"][0:2, 512 * g: 512 * (g + 1)]),
                 reads=("wte",), writes=("wte",))
        obk = "bkO" + str(g)
        orecs = []
        for hh in range(4):
            h = 4 * g + hh
            dst = pO[:, 128 * h: 128 * (h + 1)]
            orecs.append((obk, (lambda E, st_, sp_, d=dst,
                                lt=A["vr"][:, C * (t % 5) + 128 * h:
                                           C * (t % 5) + 128 * (h + 1)],
                                r=A["wtm"][:, 128 * h: 128 * (h + 1)]:
                                E.matmul(d, lt, r, start=st_, stop=sp_)),
                          (f"v{t % 5}", "wtm"), (obk,)))
            if t == 0:
                lt2 = A["vr"][0:4, C * 1 + 128 * h: C * 1 + 128 * h + 128]
                r2 = A["wte"][0:4, 128 * h: 128 * (h + 1)]
                vkey = "v1"
            elif t == NT - 1:
                lt2 = A["vx32"][0:2, 128 * h: 128 * h + 128]
                r2 = A["wte"][0:2, 128 * h: 128 * (h + 1)]
                vkey = "vx32"
            else:
                lt2 = A["vr"][0:4, C * ((t + 1) % 5) + 128 * h:
                              C * ((t + 1) % 5) + 128 * h + 128]
                r2 = A["wte"][0:4, 128 * h: 128 * (h + 1)]
                vkey = f"v{(t + 1) % 5}"
            orecs.append((obk, (lambda E, st_, sp_, d=dst, lt=lt2, r=r2:
                                E.matmul(d, lt, r, start=st_, stop=sp_)),
                          (vkey, "wte"), (obk,)))
        flags_for(orecs)
        w.op("ve", lambda E, d=A["osb"][:, 512 * g: 512 * (g + 1)],
             s_=pO[:, 512 * g: 512 * (g + 1)]: E.tensor_copy(d, s_),
             reads=(obk,), writes=("osb",))

    def st_tile(t):
        recs = []
        for nh in range(2):
            bk = "bkO" + str(nh)
            dstp = pO[:, 512 * nh: 512 * (nh + 1)]
            for cin in range(8):
                recs.append((bk, (lambda E, st_, sp_, d=dstp,
                                  lt=A["osb"][:, 128 * cin: 128 * (cin + 1)],
                                  r=A["wo"][:, C * cin + 512 * nh:
                                            C * cin + 512 * (nh + 1)]:
                                  E.matmul(d, lt, r, start=st_, stop=sp_)),
                             ("osb", "wo"), (bk,)))
        flags_for(recs)
        p = t % 2
        ks = str(p)
        dst = A["stt"][:, C * p: C * (p + 1)]
        w.op("ve", lambda E, d=dst[:, 0:512]: E.tensor_tensor(d, pO[:, 0:512], A["b2"][:, 0:512], ALU.add),
             reads=("bkO0", "b2"), writes=("stt" + ks,))
        w.op("ve", lambda E, d=dst[:, 512:1024]: E.tensor_tensor(d, pO[:, 512:1024], A["b2"][:, 512:1024], ALU.add),
             reads=("bkO1", "b2"), writes=("stt" + ks,))
        if t == 0 or t == NT - 1:
            rfk = "rf0" if t == 0 else "rfL"
            w.op("ve", lambda E, d=dst, z=A[rfk][:, 0:1]:
                 E.scalar_tensor_tensor(d, A["b2"][:], z, d, ALU.mult, ALU.add),
                 reads=("b2", "consts", "stt" + ks), writes=("stt" + ks,))
        dma("sp", st_sc[128 * t:128 * (t + 1), :], dst, "sst" + ks,
            ("stt" + ks,), ("st_sc",), ckpt=True)

    def wprT_batch(b):
        wkeys = (f"xnTs{2 * b}", f"xnTs{2 * b + 1}", "xnT", f"wprT{b}")
        for jj in range(8):
            fc = 4 * jj + b
            dma("gp", A["wprT"][:, C * fc: C * (fc + 1)],
                wprojT_d[128 * fc:128 * (fc + 1), :],
                f"swp{b}", (), wkeys)

    def xnsc_block(b):
        keys = (f"xnTs{2 * b}", f"xnTs{2 * b + 1}", "xnT")
        for c in range(8):
            dma("sp", xnT_sc[128 * c:128 * (c + 1), 1024 * b: 1024 * (b + 1)],
                A["xnT"][:, L * c + 1024 * b: L * c + 1024 * (b + 1)],
                "sxts", keys, ("xnT_sc",))

    assert NT % 4 == 0 and NS >= 2
    for j in range(2):
        ln_load(j)
    ln_stats(0)
    ln_stats(1)
    for i in range(NS + 2):
        p, qcons = i - 1, i - 2
        if i == NS:
            kr8_prod()
            vx32_prod()
        if i == NS + 1:
            wprT_batch(3)
        chunks = ([("qk", p, 0), ("qk", p, 1), ("qk", p, 2), ("qk", p, 3),
                   ("v", 4 * p), ("v", 4 * p + 1), ("v", 4 * p + 2), ("v", 4 * p + 3)]
                  if 0 <= p < NS else [])
        for j in range(4):
            tln = 4 * i + j
            t = 4 * qcons + j
            if qcons >= 0:
                scores(t, 0)
                softmax_merge(t, 0)
            for ch in chunks[2 * j: 2 * j + 1]:
                (qk_super(*ch[1:]) if ch[0] == "qk" else v_tile(ch[1]))
            if qcons >= 0:
                scores(t, 1)
                softmax_merge(t, 1)
                if t >= 1:
                    wt_and_o(t - 1, 0)
                    wt_and_o(t - 1, 1)
            for ch in chunks[2 * j + 1: 2 * j + 2]:
                (qk_super(*ch[1:]) if ch[0] == "qk" else v_tile(ch[1]))
            if qcons >= 0 and t >= 1:
                st_tile(t - 1)
            if i < NS:
                ln_apply(tln)
                if tln + 2 < NT:
                    ln_load(tln + 2)
                xn_transpose(tln)
                if tln + 2 < NT:
                    ln_stats(tln + 2)
        if i >= 2 and (i - 2) % 2 == 0 and (i - 2) // 2 < 3:
            xnsc_block((i - 2) // 2)
        if i == NS:
            xnsc_block(3)
        if i in (3, 5, 7):
            wprT_batch((i - 3) // 2)
        if i == NS:
            for c in range(6):
                dma("gp", A["wfcT"][:, FF * c: FF * (c + 1)],
                    wfcT_d[128 * c:128 * (c + 1), :],
                    "swt", (), ("wfcT6", "wq"))
    wt_and_o(NT - 1, 0)
    wt_and_o(NT - 1, 1)
    st_tile(NT - 1)

    for c in range(6, 8):
        dma("gp", A["wfcT"][:, FF * c: FF * (c + 1)], wfcT_d[128 * c:128 * (c + 1), :],
            "swt3", (), ("wfcT2", "wo"))
    w.barrier(skip_streams=("swt", "swt3", "swp0", "swp1", "swp2", "swp3"))

    NU = L // 256
    hbanks = ((pp0[:, 0:256], "pp0"), (pO[:, 0:256], "bkO0"), (pO[:, 512:768], "bkO1"))

    def c_loads(u):
        up = u % 2
        us = str(up)
        cink = "cin" + us
        for kc in range(8):
            dma("sp", A["xnTc"][:, 2048 * up + 256 * kc: 2048 * up + 256 * (kc + 1)],
                xnT_sc[128 * kc:128 * (kc + 1), 256 * u: 256 * (u + 1)],
                "sc" + us, ("xnT_sc",), (cink,), ckpt=(kc == 0))
        for ii in range(2):
            tt = 2 * u + ii
            dma("sp", A["sti"][:, 2048 * up + C * ii: 2048 * up + C * (ii + 1)],
                st_sc[128 * tt:128 * (tt + 1), :], "sc" + us,
                ("st_sc",), (cink,))
            dma("sp", A["xi"][:, 2048 * up + C * ii: 2048 * up + C * (ii + 1)],
                x_d[128 * tt:128 * (tt + 1), :], "sc" + us,
                (), (cink,))

    def c_hchunk(u, fc):
        up = u % 2
        us = str(up)
        dst, bk = hbanks[fc % 3]
        for kc in range(8):
            wkey = "wfcT6" if kc < 6 else "wfcT2"
            w.op("pe", (lambda E, d=dst,
                        lt=A["wfcT"][:, FF * kc + 128 * fc: FF * kc + 128 * (fc + 1)],
                        r=A["xnTc"][:, 2048 * up + 256 * kc: 2048 * up + 256 * (kc + 1)],
                        s=(kc == 0), p_=(kc == 7):
                        E.matmul(d, lt, r, start=s, stop=p_)),
                 reads=(wkey, "cin" + us), writes=(bk,))
        hr = A["hrl"][:, 256 * (fc % 2): 256 * (fc % 2 + 1)]
        w.op("ae", lambda E, d=hr, i=dst: E.activation(d, i, AF.Relu),
             reads=(bk,), writes=(f"hrl{fc % 2}",))
        w.op("ae", lambda E, d=A["hsb"][:, 256 * fc: 256 * (fc + 1)], i=hr:
             E.activation(d, i, AF.Square),
             reads=(f"hrl{fc % 2}",), writes=("hsb",))

    def c_y_snn(u, ii):
        pshalves = (pS0, pS1) if ii == 0 else (pS2, pp1)
        bks = ("bkS0", "bkS1") if ii == 0 else ("bkS2", "pp1")
        recs = []
        fcs = (sorted(range(32), key=lambda fc: (fc % 4, fc)) if u < 2
               else list(range(32)))
        for fc in fcs:
            lhsT = A["hsb"][:, 256 * fc + 128 * ii: 256 * fc + 128 * (ii + 1)]
            for nh in range(2):
                rhs = A["wprT"][:, C * fc + 512 * nh: C * fc + 512 * (nh + 1)]
                dst = pshalves[nh][:]
                recs.append((bks[nh], (lambda E, s, p_, d=dst, lt=lhsT, r=rhs:
                                       E.matmul(d, lt, r, start=s, stop=p_)),
                             ("hsb", f"wprT{fc % 4}"), (bks[nh],)))
        flags_for(recs)
        ac = A["acc"][:, C * ii: C * (ii + 1)]
        for nh in range(2):
            ah = ac[:, 512 * nh: 512 * (nh + 1)]
            w.op("ve", lambda E, o=ah, y=pshalves[nh]:
                 E.tensor_single_scalar(o, y[:], THS[0], ALU.is_gt),
                 reads=(bks[nh],), writes=(f"acc{ii}",))
            for jj in range(1, 5):
                w.op("ve", lambda E, o=ah, y=pshalves[nh], th=THS[jj]:
                     E.scalar_tensor_tensor(o, y[:], th, o, ALU.is_gt, ALU.add),
                     reads=(bks[nh], f"acc{ii}"), writes=(f"acc{ii}",))

    def c_tail(u, ii):
        up = u % 2
        us = str(up)
        tt = 2 * u + ii
        ac = A["acc"][:, C * ii: C * (ii + 1)]
        g = A["gsb"][:, C * ii: C * (ii + 1)]
        w.op("ae", lambda E, o=g, i_=ac: E.activation(o, i_, AF.Silu, scale=0.2),
             reads=(f"acc{ii}",), writes=(f"gsb{ii}",))
        w.op("ve", lambda E, o=g, a=A["sti"][:, 2048 * up + C * ii: 2048 * up + C * (ii + 1)]:
             E.tensor_tensor(o, a, o, ALU.mult),
             reads=("cin" + us, f"gsb{ii}"), writes=(f"gsb{ii}",))
        ot = A["ot"][:, C * ii: C * (ii + 1)]
        w.op("ve", lambda E, o=ot, a=A["xi"][:, 2048 * up + C * ii: 2048 * up + C * (ii + 1)], b=g:
             E.tensor_tensor(o, a, b, ALU.add),
             reads=("cin" + us, f"gsb{ii}"), writes=(f"ot{ii}",))
        dma("sp", out_d[128 * tt:128 * (tt + 1), :], ot, "so" + str(tt % 2),
            (f"ot{ii}",), ("outd",), ckpt=True)

    c_loads(0)
    for u in range(NU):
        for fc in range(32):
            c_hchunk(u, fc)
            if u > 0 and fc == 3:
                c_tail(u - 1, 0)
            if u > 0 and fc == 7:
                c_tail(u - 1, 1)
            if fc == 11 and u + 1 < NU:
                c_loads(u + 1)
        c_y_snn(u, 0)
        c_y_snn(u, 1)
    c_tail(NU - 1, 0)
    c_tail(NU - 1, 1)

    w.op("sp", lambda E: E.nop(), reads=("outd",))
    final_waits = {"so0": w.cnt.get("so0", 0), "so1": w.cnt.get("so1", 0)}
    w.prog["sp"][-1]["waits"].update(final_waits)

    sems = {}
    import contextlib
    stack = contextlib.ExitStack()
    for name in set(ENGS) | set(w.cnt.keys()) | set(w.stream_eng.keys()):
        sems[name] = stack.enter_context(nc.semaphore("sem_" + name))

    block = stack.enter_context(nc.Block())

    def replay(eng_name):
        def body(E):
            watermark = {}
            for rec in w.prog[eng_name]:
                for s, v in sorted(rec["waits"].items()):
                    if v > 0 and watermark.get(s, -1) < v:
                        E.wait_ge(sems[s], v)
                        watermark[s] = v
                instr = rec["fn"](E)
                if rec["inc"] is not None:
                    instr = instr.then_inc(sems[rec["inc"][0]], rec["inc"][1])
                if rec.get("inc2") is not None:
                    instr.then_inc(sems[rec["inc2"][0]], rec["inc2"][1])
        return body

    block.tensor(replay("pe"))
    block.vector(replay("ve"))
    block.scalar(replay("ae"))
    block.sync(replay("sp"))
    block.gpsimd(replay("gp"))
    stack.close()
    return nc


_CACHE = {}


def kernel(L0_attn_x, L0_attn_w_qkv, L0_attn_w_out, L0_attn_b_out,
           L0_ffn_W_fc, L0_ffn_W_proj):
    x = np.ascontiguousarray(np.asarray(L0_attn_x, np.float32))
    wqkv = np.ascontiguousarray(np.asarray(L0_attn_w_qkv, np.float32))
    wout = np.ascontiguousarray(np.asarray(L0_attn_w_out, np.float32))
    bout = np.ascontiguousarray(np.asarray(L0_attn_b_out, np.float32)).reshape(1, C)
    wfcT = np.ascontiguousarray(np.asarray(L0_ffn_W_fc, np.float32).T)
    wprojT = np.ascontiguousarray(np.asarray(L0_ffn_W_proj, np.float32).T)
    B, L, _ = x.shape
    if "nc" not in _CACHE:
        _CACHE["nc"] = build_program(L)
    nc = _CACHE["nc"]
    in_maps = [{"x": x[b], "w_qkv": wqkv, "w_out": wout, "b_out": bout,
                "w_fcT": wfcT, "w_projT": wprojT} for b in range(B)]
    res = run_bass_kernel_spmd(nc, in_maps, core_ids=list(range(B)))
    return np.stack([res.results[b]["out"] for b in range(B)], axis=0)


# version 24
# speedup vs baseline: 1.0606x; 1.0606x over previous
import numpy as np
import ml_dtypes

import concourse.bass as bass
import concourse.mybir as mybir
from concourse.bass_utils import run_bass_kernel_spmd

F32 = mybir.dt.float32
BF16 = mybir.dt.bfloat16
AF = mybir.ActivationFunctionType
ALU = mybir.AluOpType
AX = mybir.AxisListType
BF = ml_dtypes.bfloat16

C = 1024
NH = 8
DH = 128
FF = 4096
MASKV = 1280.0
SCALE = 1.0 / 32.0
THS = (0.24419428096993973, 0.4639691338428854, 0.6861859295255306,
       0.9059607823984763, 1.0839784132255623)

ENGS = ("pe", "ve", "ae", "sp", "gp")


def _build_masks():
    mA = np.full((128, 132), -MASKV, np.float32)
    mB = np.full((128, 132), -MASKV, np.float32)
    for p in range(128):
        blk = 4 * (p // 4)
        mA[p, blk + 2: blk + 6] = 0.0
        jB = blk if p % 4 < 2 else blk + 4
        mB[p, jB: jB + 4] = 0.0
    return mA.astype(BF), mB.astype(BF)


class Walk:

    def __init__(self):
        self.prog = {e: [] for e in ENGS}
        self.cnt = {}
        self.wmarks = {}
        self.rmarks = {}
        self.pending = {e: [] for e in ENGS}
        self.barrier_waits = {}
        self.stream_eng = {}

    def barrier(self, skip_streams=()):
        vals = {}
        for e in ENGS:
            if self.prog[e]:
                vals[e] = self._mark_val(e)
        for s, eng in self.stream_eng.items():
            if s not in skip_streams:
                vals[s] = self.cnt.get(s, 0)
        for e in ENGS:
            self.barrier_waits[e] = dict(vals)

    def _mark_val(self, eng):
        if self.pending[eng] or (self.prog[eng] and self.prog[eng][-1]["inc"] is None):
            return self._flush(eng)
        return self.cnt.get(eng, 0)


class Walk2(Walk):
    def op(self, eng, fn, reads=(), writes=(), dma=None, ckpt=False, tag=""):
        waits = {}
        for b in reads:
            for e2 in ENGS:
                if b in self.pending[e2]:
                    self._flush(e2)
            for s, v in self.wmarks.get(b, {}).items():
                waits[s] = max(waits.get(s, 0), v)
        for b in writes:
            for e2 in ENGS:
                if b in self.pending[e2] and e2 != eng:
                    self._flush(e2)
            for s, v in self.wmarks.get(b, {}).items():
                if s != eng:
                    waits[s] = max(waits.get(s, 0), v)
            rm = self.rmarks.get(b, {})
            for s in list(rm):
                v = rm[s]
                if v is None:
                    v = self._flush(s)
                    rm[s] = v
                if s != eng:
                    waits[s] = max(waits.get(s, 0), v)
        if dma is not None:
            self.stream_eng[dma] = eng
            if ckpt:
                prev = self.cnt.get(dma, 0)
                if prev:
                    waits[dma] = max(waits.get(dma, 0), prev)
            v = self.cnt.get(dma, 0) + 16
            self.cnt[dma] = v
            inc = (dma, 16)
            for b in writes:
                self.wmarks[b] = {dma: v}
                self.rmarks[b] = {}
            for b in reads:
                self.rmarks.setdefault(b, {})[dma] = v
        else:
            inc = None
            for b in writes:
                self.pending[eng].append(b)
                self.wmarks[b] = {}
                self.rmarks[b] = {}
            for b in reads:
                self.rmarks.setdefault(b, {})[eng] = None
        bw = self.barrier_waits.pop(eng, None)
        if bw:
            for s, v in bw.items():
                waits[s] = max(waits.get(s, 0), v)
        self.prog[eng].append(dict(fn=fn, waits=waits, inc=inc, tag=tag))
        return len(self.prog[eng]) - 1

    def _flush(self, eng):
        ops = self.prog[eng]
        if not ops:
            return self.cnt.get(eng, 0)
        rec = ops[-1]
        if rec["inc"] is None and rec.get("_dma", False) is False:
            v = self.cnt.get(eng, 0) + 1
            self.cnt[eng] = v
            rec["inc"] = (eng, 1)
            for b in self.pending[eng]:
                self.wmarks.setdefault(b, {})[eng] = v
            self.pending[eng] = []
            for b, rm in self.rmarks.items():
                if rm.get(eng, 0) is None:
                    rm[eng] = v
            return v
        if self.pending[eng] or any(rm.get(eng, 0) is None
                                    for rm in self.rmarks.values()):
            if rec["inc"] is not None and rec["inc"][0] != eng:
                extra = rec.get("inc2")
                if extra is None:
                    v = self.cnt.get(eng, 0) + 1
                    self.cnt[eng] = v
                    rec["inc2"] = (eng, 1)
                for b in self.pending[eng]:
                    self.wmarks.setdefault(b, {})[eng] = self.cnt[eng]
                self.pending[eng] = []
                for b, rm in self.rmarks.items():
                    if rm.get(eng, 0) is None:
                        rm[eng] = self.cnt[eng]
                return self.cnt[eng]
        return self.cnt.get(eng, 0)


def build_program(L=4096):
    NT = L // 128
    NS = L // 512
    nc = bass.Bass("TRN2", target_bir_lowering=False)

    x_d = nc.dram_tensor("x", [L, C], F32, kind="ExternalInput").ap()
    wqkv_d = nc.dram_tensor("w_qkv", [C, 3 * C], F32, kind="ExternalInput").ap()
    wout_d = nc.dram_tensor("w_out", [C, C], F32, kind="ExternalInput").ap()
    bout_d = nc.dram_tensor("b_out", [1, C], F32, kind="ExternalInput").ap()
    wfcT_d = nc.dram_tensor("w_fcT", [C, FF], F32, kind="ExternalInput").ap()
    wprojT_d = nc.dram_tensor("w_projT", [FF, C], F32, kind="ExternalInput").ap()
    out_d = nc.dram_tensor("out", [L, C], F32, kind="ExternalOutput").ap()

    xnT_sc = nc.dram_tensor("xnT_sc", [C, L], BF16, kind="Internal").ap()
    st_sc = nc.dram_tensor("st_sc", [L, C], BF16, kind="Internal").ap()

    mA_np, mB_np = _build_masks()
    eye_h = nc.inline_tensor(np.eye(128, dtype=BF), "c_eye")
    ones_h = nc.inline_tensor(np.ones((1, 128), np.float32), "c_ones")
    mA_h = nc.inline_tensor(mA_np, "c_mA")
    mB_h = nc.inline_tensor(mB_np, "c_mB")
    rz0 = np.ones((128, 1), np.float32); rz0[0:2] = 0.0
    rzL = np.ones((128, 1), np.float32); rzL[126:128] = 0.0
    rf0 = np.zeros((128, 1), np.float32); rf0[0:2] = -0.5
    rfL = np.zeros((128, 1), np.float32); rfL[126:128] = -0.5
    zer_h = nc.inline_tensor(np.zeros((128, 1), np.float32), "c_zero")
    eps_h = nc.inline_tensor(np.full((128, 1), 1e-5, np.float32), "c_eps")
    z16_h = nc.inline_tensor(np.zeros((128, 16), BF), "c_z16")
    rz0_h = nc.inline_tensor(rz0, "c_rz0")
    rzL_h = nc.inline_tensor(rzL, "c_rzL")
    rf0_h = nc.inline_tensor(rf0, "c_rf0")
    rfL_h = nc.inline_tensor(rfL, "c_rfL")

    A = {}
    SB_BASE = 16448
    def sb(name, shape, dtype, off):
        A[name] = nc.alloc_sbuf_tensor_at(name, shape, dtype, offset=SB_BASE + off)
        return A[name]

    sb("z16", [128, 16], BF16, -64)
    sb("eye", [128, 128], BF16, 0)
    sb("ones1", [1, 128], F32, 256)
    sb("mA", [128, 132], BF16, 768)
    sb("mB", [128, 132], BF16, 1056)
    sb("rz0", [128, 1], F32, 1344)
    sb("rzL", [128, 1], F32, 1376)
    sb("rf0", [128, 1], F32, 1408)
    sb("rfL", [128, 1], F32, 1440)
    sb("zero", [128, 1], F32, 1472)
    sb("eps", [128, 1], F32, 1504)
    sb("bout", [1, C], F32, 1536)
    sb("b2", [128, C], F32, 5632)
    nc.const_aps.register(F32, 0.0, A["zero"].ap())
    WQ_OFF = 9728
    sb("wq", [128, 8 * 3 * C], BF16, WQ_OFF)
    sb("wo", [128, 8 * C], BF16, WQ_OFF + 49152)
    sb("wfcT", [128, 8 * FF], BF16, WQ_OFF)
    XNT_OFF = WQ_OFF + 65536
    sb("xnT", [128, 8 * L], BF16, XNT_OFF)
    sb("wprT", [128, 32 * C], BF16, XNT_OFF)
    AR = XNT_OFF + 16 * L
    sb("qr", [128, 2 * NH * 512], BF16, AR)
    sb("kr", [128, 2 * NH * 512], BF16, AR + 16384)
    sb("vr", [128, 5 * C], BF16, AR + 32768)
    sb("xt", [128, 2 * C], F32, AR + 43008)
    sb("vx32", [2, C], BF16, AR + 43008)
    sb("kr8", [128, 16], BF16, AR + 45056)
    sb("xnb", [128, 2 * C], BF16, AR + 51200)
    sb("expA", [128, 528], BF16, AR + 55296)
    sb("expB", [128, 528], BF16, AR + 56352)
    sb("wb", [128, 4 * 528], BF16, AR + 57408)
    sb("stt", [128, 2 * C], BF16, AR + 61632)
    sb("wtm", [128, 1024], BF16, AR + 65728)
    sb("osb", [128, 1024], BF16, AR + 67776)
    sb("wte", [4, 1024], BF16, AR + 69824)
    sb("astat", [128, 16], F32, AR + 71872)
    sb("rsb", [128, 32], F32, AR + 71936)
    assert SB_BASE + AR + 72064 <= 229368, f"sbuf overflow B: {SB_BASE + AR + 72064}"
    sb("xnTc", [128, 4096], BF16, AR)
    sb("hsb", [128, 8192], BF16, AR + 8192)
    sb("hrl", [128, 512], BF16, AR + 24576)
    sb("acc", [128, 2048], F32, AR + 25600)
    sb("gsb", [128, 2048], BF16, AR + 33792)
    sb("sti", [128, 4096], BF16, AR + 37888)
    sb("xi", [128, 4096], F32, AR + 46080)
    sb("ot", [128, 2048], F32, AR + 62464)
    assert SB_BASE + AR + 70656 <= 229376

    pp0 = nc.alloc_psum_tensor("pp0", [128, 512], F32)
    pp1 = nc.alloc_psum_tensor("pp1", [128, 512], F32)
    pS0 = nc.alloc_psum_tensor("pS0", [128, 512], F32)
    pS1 = nc.alloc_psum_tensor("pS1", [128, 512], F32)
    pS2 = nc.alloc_psum_tensor("pS2", [128, 512], F32)
    pT = nc.alloc_psum_tensor("pT", [128, 1024], BF16)
    pO = nc.alloc_psum_tensor("pO", [128, 1024], F32)

    w = Walk2()
    rotc = [0]

    def rot():
        k = rotc[0] & 1
        rotc[0] += 1
        return (pp0, "pp0") if k == 0 else (pp1, "pp1")

    def dma(eng, dst, src, stream, reads, writes, ckpt=False, transpose=False):
        if transpose:
            w.op(eng, lambda E, d=dst, s=src: E.dma_start_transpose(d, s),
                 reads=reads, writes=writes, dma=stream, ckpt=ckpt)
        else:
            w.op(eng, lambda E, d=dst, s=src: E.dma_start(d, s),
                 reads=reads, writes=writes, dma=stream, ckpt=ckpt)

    def flags_for(recs):
        first = {}
        last = {}
        for i, (bank, _fn, _r, _w) in enumerate(recs):
            if bank not in first:
                first[bank] = i
            last[bank] = i
        for i, (bank, fn, rd, wr) in enumerate(recs):
            st = (i == first[bank])
            sp_ = (i == last[bank])
            w.op("pe", (lambda E, f=fn, a=st, b=sp_: f(E, a, b)),
                 reads=rd, writes=wr)

    for name, hdl in (("eye", eye_h), ("ones1", ones_h), ("mA", mA_h),
                      ("mB", mB_h), ("rz0", rz0_h), ("rzL", rzL_h),
                      ("rf0", rf0_h), ("rfL", rfL_h), ("zero", zer_h),
                      ("eps", eps_h), ("z16", z16_h)):
        dma("sp", A[name][:], hdl.ap(), "smc", (), ("consts", "bout"))
    dma("sp", A["bout"][:], bout_d, "smc", (), ("consts", "bout"))
    for _e in ENGS:
        w.barrier_waits[_e] = {"smc": w.cnt["smc"]}
    for kc in range(8):
        dma("gp", A["wq"][:, 3 * C * kc: 3 * C * (kc + 1)],
            wqkv_d[128 * kc:128 * (kc + 1), :], "swq", (), ("wq",))
    for kc in range(8):
        dma("gp", A["wo"][:, C * kc: C * (kc + 1)],
            wout_d[128 * kc:128 * (kc + 1), :], "swo", (), ("wo",))
    w.op("pe", lambda E: E.matmul(pp0[:], A["ones1"][:], A["bout"][:, 0:512],
                                  start=True, stop=True),
         reads=("bout", "consts"), writes=("pp0",))
    w.op("pe", lambda E: E.matmul(pp1[:], A["ones1"][:], A["bout"][:, 512:1024],
                                  start=True, stop=True),
         reads=("bout", "consts"), writes=("pp1",))
    w.op("ae", lambda E: E.activation(A["b2"][:, 0:512], pp0[:], AF.Copy, scale=2.0),
         reads=("pp0",), writes=("b2",))
    w.op("ae", lambda E: E.activation(A["b2"][:, 512:1024], pp1[:], AF.Copy, scale=2.0),
         reads=("pp1",), writes=("b2",))
    w.op("ve", lambda E: E.tensor_copy(
        A["kr"][:, 0: NH * 512].rearrange("p (h c) -> p h c", h=NH)[:, :, 0:2],
        A["z16"][:].rearrange("p (h c) -> p h c", h=NH)),
         reads=("consts",), writes=("k0",))

    inv_c = 1.0 / C

    def ln_load(t):
        p = t % 2
        dma("sp", A["xt"][:, C * p: C * (p + 1)], x_d[128 * t:128 * (t + 1), :],
            "sx" + str(p), (), (f"xt{p}",), ckpt=True)

    def ln_stats(t):
        p = t % 2
        xt = A["xt"][:, C * p: C * (p + 1)]
        xn = A["xnb"][:, C * p: C * (p + 1)]
        st = A["astat"][:, 8 * p: 8 * p + 8]
        w.op("ve", lambda E, o=st, i=xt: E.tensor_reduce(o[:, 0:1], i, AX.X, ALU.add),
             reads=(f"xt{p}",), writes=(f"astat{p}",))
        w.op("ae", lambda E, i=xt, s=st, d=xn: E.activation(d, i, AF.Square,
                                                            accum_out=s[:, 1:2]),
             reads=(f"xt{p}",), writes=(f"astat{p}", f"xnb{p}"))
        w.op("ve", lambda E, s=st: E.tensor_scalar(s[:, 2:3], s[:, 0:1], inv_c, None, ALU.mult),
             reads=(f"astat{p}",), writes=(f"astat{p}",))
        w.op("ve", lambda E, s=st: E.tensor_tensor(s[:, 3:4], s[:, 2:3], s[:, 2:3], ALU.mult),
             reads=(f"astat{p}",), writes=(f"astat{p}",))
        w.op("ve", lambda E, s=st: E.tensor_scalar(s[:, 3:4], s[:, 3:4], -1e-5, None, ALU.add),
             reads=(f"astat{p}",), writes=(f"astat{p}",))
        w.op("ve", lambda E, s=st: E.scalar_tensor_tensor(
            s[:, 4:5], s[:, 1:2], inv_c, s[:, 3:4], ALU.mult, ALU.subtract),
             reads=(f"astat{p}",), writes=(f"astat{p}",))
        w.op("ae", lambda E, s=st: E.activation(s[:, 5:6], s[:, 4:5], AF.Ln),
             reads=(f"astat{p}",), writes=(f"astat{p}",))
        w.op("ae", lambda E, s=st: E.activation(s[:, 6:7], s[:, 5:6], AF.Exp,
                                                scale=-0.5),
             reads=(f"astat{p}",), writes=(f"astat{p}",))
        w.op("ve", lambda E, s=st: E.scalar_tensor_tensor(
            s[:, 7:8], s[:, 2:3], -1.0, s[:, 6:7], ALU.mult, ALU.mult),
             reads=(f"astat{p}",), writes=(f"astat{p}",))

    def ln_apply(t):
        p = t % 2
        xt = A["xt"][:, C * p: C * (p + 1)]
        xn = A["xnb"][:, C * p: C * (p + 1)]
        st = A["astat"][:, 8 * p: 8 * p + 8]
        w.op("ae", lambda E, i=xt, s=st, d=xn: E.activation(
            d, i, AF.Identity, scale=s[:, 6:7], bias=s[:, 7:8]),
             reads=(f"xt{p}", f"astat{p}"), writes=(f"xnb{p}",))

    def xn_transpose(t):
        p4 = t % 2
        bkey = "pT"
        bb = pT[:]
        for c in range(8):
            w.op("pe", lambda E, d=bb[:, 128 * c: 128 * (c + 1)],
                 i=A["xnb"][:, C * p4 + 128 * c: C * p4 + 128 * (c + 1)]:
                 E.transpose(d, i, A["eye"][:]),
                 reads=(f"xnb{p4}", "consts"), writes=(bkey,))
        dst = A["xnT"][:].rearrange("p (c l) -> p c l", c=8)[
            :, :, 128 * t: 128 * (t + 1)]
        src = bb.rearrange("p (c l) -> p c l", c=8)
        if t % 2 == 0:
            w.op("ve", lambda E, d=dst, s=src: E.tensor_copy(d, s),
                 reads=(bkey,), writes=(f"xnTs{t // 4}", "xnT"))
        else:
            w.op("ae", lambda E, d=dst, s=src: E.activation(d, s, AF.Copy),
                 reads=(bkey,), writes=(f"xnTs{t // 4}", "xnT"))

    def qk_super(s, part):
        isq = part < 2
        hs = 4 * (part % 2)
        sl = s % 2
        if isq:
            tok0, N, doff = 512 * s, 512, 0
        elif s > 0:
            tok0, N, doff = 512 * s - 2, 512, 0
        else:
            tok0, N, doff = 0, 510, 2
        qkoff = 0 if isq else C
        dst_sb = A["qr"] if isq else A["kr"]
        key = (f"q{sl}" if isq else f"k{sl}")
        rds = ("wq", f"xnTs{s}") + ((f"xnTs{s - 1}",) if (not isq and s > 0) else ())
        for h in range(hs, hs + 4):
            bank, bkey = rot()
            for kc in range(8):
                w.op("pe", (lambda E, d=bank[:, doff: doff + N],
                            lt=A["wq"][:, 3 * C * kc + qkoff + 128 * h:
                                       3 * C * kc + qkoff + 128 * (h + 1)],
                            r=A["xnT"][:, L * kc + tok0: L * kc + tok0 + N],
                            st_=(kc == 0), sp_=(kc == 7):
                            E.matmul(d, lt, r, start=st_, stop=sp_)),
                     reads=rds, writes=(bkey,))
            w.op("ae", lambda E, d=dst_sb[:, 4096 * sl + 512 * h + doff:
                                          4096 * sl + 512 * h + doff + N],
                 s_=bank[:, doff: doff + N]: E.activation(d, s_, AF.Copy),
                 reads=(bkey,), writes=(key,))

    def v_tile(u):
        vstart = 0 if u == 0 else 128 * u - 2
        sup = vstart // 512
        rds = ("wq", f"xnTs{u // 4}") + ((f"xnTs{sup}",) if sup != u // 4 else ())
        for nh in range(2):
            bank, bkey = rot()
            for kc in range(8):
                w.op("pe", (lambda E, d=bank[:],
                            lt=A["xnT"][:, L * kc + vstart: L * kc + vstart + 128],
                            r=A["wq"][:, 3 * C * kc + 2 * C + 512 * nh:
                                      3 * C * kc + 2 * C + 512 * (nh + 1)],
                            st_=(kc == 0), sp_=(kc == 7):
                            E.matmul(d, lt, r, start=st_, stop=sp_)),
                     reads=rds, writes=(bkey,))
            w.op("ae", lambda E, d=A["vr"][:, C * (u % 5) + 512 * nh:
                                           C * (u % 5) + 512 * (nh + 1)],
                 s_=bank[:]: E.activation(d, s_, AF.Copy),
                 reads=(bkey,), writes=(f"v{u % 5}",))

    def kr8_prod():
        bank, bkey = rot()
        recs = []
        for h in range(8):
            for kc in range(8):
                recs.append((bkey, (lambda E, st_, sp_,
                                    d=bank[:, 2 * h: 2 * h + 2],
                                    lt=A["wq"][:, 3 * C * kc + C + 128 * h:
                                               3 * C * kc + C + 128 * (h + 1)],
                                    r=A["xnT"][:, L * kc + L - 2: L * kc + L]:
                                    E.matmul(d, lt, r, start=st_, stop=sp_)),
                             ("wq", f"xnTs{NS - 1}"), (bkey,)))
        flags_for(recs)
        w.op("ve", lambda E, d=A["kr8"][:], s_=bank[:, 0:16]: E.tensor_copy(d, s_),
             reads=(bkey,), writes=("kr8",))

    def vx32_prod():
        for nh in range(2):
            bank, bkey = rot()
            for kc in range(8):
                w.op("pe", (lambda E, d=bank[0:2, :],
                            lt=A["xnT"][:, L * kc + L - 2: L * kc + L],
                            r=A["wq"][:, 3 * C * kc + 2 * C + 512 * nh:
                                      3 * C * kc + 2 * C + 512 * (nh + 1)],
                            st_=(kc == 0), sp_=(kc == 7):
                            E.matmul(d, lt, r, start=st_, stop=sp_)),
                     reads=("wq", f"xnTs{NS - 1}"), writes=(bkey,))
            w.op("ve", lambda E, d=A["vx32"][0:2, 512 * nh: 512 * (nh + 1)],
                 s_=bank[0:2, :]: E.tensor_copy(d, s_),
                 reads=(bkey,), writes=("vx32",))

    SBANKS = ((pS0, "bkS0"), (pS1, "bkS1"), (pS2, "bkS2"))

    def scores(t, g):
        s_sup, j = t // 4, t % 4
        ksl = (s_sup % 2)
        recs = []
        for hh in range(4):
            h = 4 * g + hh
            lhsT = A["qr"][:, 4096 * ksl + 512 * h + 128 * j:
                           4096 * ksl + 512 * h + 128 * j + 128]
            for seti, mkey in ((0, "mA"), (1, "mB")):
                r = 4 * seti + hh
                ps, bk = SBANKS[r // 3]
                reg0 = 132 * (r % 3)
                segs = []
                if j < 3:
                    segs.append((A["kr"][:, 4096 * ksl + 512 * h + 128 * j:
                                         4096 * ksl + 512 * h + 128 * j + 132],
                                 0, 132, f"k{ksl}"))
                else:
                    segs.append((A["kr"][:, 4096 * ksl + 512 * h + 384:
                                         4096 * ksl + 512 * h + 512],
                                 0, 128, f"k{ksl}"))
                    if t < NT - 1:
                        segs.append((A["kr"][:, 4096 * ((s_sup + 1) % 2) + 512 * h:
                                             4096 * ((s_sup + 1) % 2) + 512 * h + 4],
                                     128, 132, f"k{(s_sup + 1) % 2}"))
                    else:
                        segs.append((A["kr8"][:, 2 * h: 2 * h + 2],
                                     128, 130, "kr8"))
                recs.append((bk, (lambda E, st_, sp_, d=ps[:, reg0: reg0 + 132],
                                  r=A[mkey][:]:
                                  E.matmul(d, A["eye"][:], r, start=st_, stop=sp_)),
                             ("consts",), (bk,)))
                for rhs, c0, c1, kkey in segs:
                    dst = ps[:, reg0 + c0: reg0 + c1]
                    recs.append((bk, (lambda E, st_, sp_, d=dst, lt=lhsT, r=rhs:
                                      E.matmul(d, lt, r, start=st_, stop=sp_)),
                                 (f"q{ksl}", kkey), (bk,)))
        flags_for(recs)

    def softmax_merge(t, g):
        s = g
        ss = str(s)
        s2 = 2 * (t % 2) + g
        expA = A["expA"][:, 0:528]
        expB = A["expB"][:, 0:528]
        rs = A["rsb"]
        wkey = f"wb{t % 2}{g}"
        w.op("ae", lambda E, o=expA[:, 0:396], p_=pS0:
             E.activation(o, p_[:, 0:396], AF.Exp, scale=SCALE),
             reads=("bkS0",), writes=("expA",))
        w.op("ae", lambda E, o=expA[:, 396:528], p_=pS1:
             E.activation(o, p_[:, 0:132], AF.Exp, scale=SCALE),
             reads=("bkS1",), writes=("expA",))
        w.op("ae", lambda E, o=expB[:, 0:264], p_=pS1:
             E.activation(o, p_[:, 132:396], AF.Exp, scale=SCALE),
             reads=("bkS1",), writes=("expB",))
        w.op("ae", lambda E, o=expB[:, 264:528], p_=pS2:
             E.activation(o, p_[:, 0:264], AF.Exp, scale=SCALE),
             reads=("bkS2",), writes=("expB",))
        w.op("ve", lambda E, i=expA, o=rs[:, 16 * s: 16 * s + 4]:
             E.tensor_reduce(o, i.rearrange("p (h j) -> p h j", h=4), AX.X, ALU.add),
             reads=("expA",), writes=("rs" + ss,))
        w.op("ve", lambda E, i=expB, o=rs[:, 16 * s + 4: 16 * s + 8]:
             E.tensor_reduce(o, i.rearrange("p (h j) -> p h j", h=4), AX.X, ALU.add),
             reads=("expB",), writes=("rs" + ss,))
        w.op("ve", lambda E, o=rs[:, 16 * s + 8: 16 * s + 16],
             i=rs[:, 16 * s: 16 * s + 8]: E.reciprocal(o, i),
             reads=("rs" + ss,), writes=("rs" + ss,))
        if t == 0 or t == NT - 1:
            rzk = "rz0" if t == 0 else "rzL"
            w.op("ve", lambda E, o=rs[:, 16 * s + 12: 16 * s + 16], z=A[rzk][:, 0:1]:
                 E.tensor_scalar(o, o, z, None, ALU.mult),
                 reads=("rs" + ss, "consts"), writes=("rs" + ss,))
        for hh in range(4):
            wdst = A["wb"][:, 528 * s2 + 132 * hh: 528 * s2 + 132 * (hh + 1)]
            w.op("ve", lambda E, o=wdst, i=expA[:, 132 * hh: 132 * (hh + 1)],
                 r=rs[:, 16 * s + 8 + hh: 16 * s + 9 + hh]:
                 E.tensor_scalar(o, i, r, None, ALU.mult),
                 reads=("expA", "rs" + ss), writes=(wkey,))
            w.op("ve", lambda E, o=wdst, i=expB[:, 132 * hh: 132 * (hh + 1)],
                 r=rs[:, 16 * s + 12 + hh: 16 * s + 13 + hh]:
                 E.scalar_tensor_tensor(o, i, r, o, ALU.mult, ALU.add),
                 reads=("expB", "rs" + ss, wkey), writes=(wkey,))

    def wt_and_o(t, g):
        s = g
        s2 = 2 * (t % 2) + g
        ss = f"wb{t % 2}{g}"
        shift = 2 if t == 0 else 0
        bkey = "pT"
        bb = pT[:]
        for hh in range(4):
            w.op("pe", lambda E, d=bb[:, 128 * hh: 128 * (hh + 1)],
                 i=A["wb"][:, 528 * s2 + 132 * hh + shift: 528 * s2 + 132 * hh + shift + 128]:
                 E.transpose(d, i, A["eye"][:]),
                 reads=(ss, "consts"), writes=(bkey,))
        for hh in range(4):
            w.op("pe", lambda E, d=bb[0:4, 512 + 128 * hh: 512 + 128 * (hh + 1)],
                 i=A["wb"][:, 528 * s2 + 132 * hh + 128: 528 * s2 + 132 * (hh + 1)]:
                 E.transpose(d, i, A["eye"][:]),
                 reads=(ss, "consts"), writes=(bkey,))
        w.op("ve", lambda E, d=A["wtm"][:, 512 * g: 512 * (g + 1)], s_=bb[:, 0:512]:
             E.tensor_copy(d, s_),
             reads=(bkey,), writes=("wtm",))
        w.op("ve", lambda E, d=A["wte"][0:4, 512 * g: 512 * (g + 1)],
             s_=bb[0:4, 512:1024]: E.tensor_copy(d, s_),
             reads=(bkey,), writes=("wte",))
        if t == 0:
            w.op("ve", lambda E, d=A["wte"][0:2, 512 * g: 512 * (g + 1)],
                 i=A["wtm"][0:2, 512 * g: 512 * (g + 1)]:
                 E.tensor_scalar(d, i, 0.0, None, ALU.mult),
                 reads=("wte", "wtm"), writes=("wte",))
        obk = "bkO" + str(g)
        orecs = []
        for hh in range(4):
            h = 4 * g + hh
            dst = pO[:, 128 * h: 128 * (h + 1)]
            orecs.append((obk, (lambda E, st_, sp_, d=dst,
                                lt=A["vr"][:, C * (t % 5) + 128 * h:
                                           C * (t % 5) + 128 * (h + 1)],
                                r=A["wtm"][:, 128 * h: 128 * (h + 1)]:
                                E.matmul(d, lt, r, start=st_, stop=sp_)),
                          (f"v{t % 5}", "wtm"), (obk,)))
            if t == 0:
                lt2 = A["vr"][0:4, C * 1 + 128 * h: C * 1 + 128 * h + 128]
                r2 = A["wte"][0:4, 128 * h: 128 * (h + 1)]
                vkey = "v1"
            elif t == NT - 1:
                lt2 = A["vx32"][0:2, 128 * h: 128 * h + 128]
                r2 = A["wte"][0:2, 128 * h: 128 * (h + 1)]
                vkey = "vx32"
            else:
                lt2 = A["vr"][0:4, C * ((t + 1) % 5) + 128 * h:
                              C * ((t + 1) % 5) + 128 * h + 128]
                r2 = A["wte"][0:4, 128 * h: 128 * (h + 1)]
                vkey = f"v{(t + 1) % 5}"
            orecs.append((obk, (lambda E, st_, sp_, d=dst, lt=lt2, r=r2:
                                E.matmul(d, lt, r, start=st_, stop=sp_)),
                          (vkey, "wte"), (obk,)))
        flags_for(orecs)
        w.op("ve", lambda E, d=A["osb"][:, 512 * g: 512 * (g + 1)],
             s_=pO[:, 512 * g: 512 * (g + 1)]: E.tensor_copy(d, s_),
             reads=(obk,), writes=("osb",))

    def st_tile(t):
        recs = []
        for nh in range(2):
            bk = "bkO" + str(nh)
            dstp = pO[:, 512 * nh: 512 * (nh + 1)]
            for cin in range(8):
                recs.append((bk, (lambda E, st_, sp_, d=dstp,
                                  lt=A["osb"][:, 128 * cin: 128 * (cin + 1)],
                                  r=A["wo"][:, C * cin + 512 * nh:
                                            C * cin + 512 * (nh + 1)]:
                                  E.matmul(d, lt, r, start=st_, stop=sp_)),
                             ("osb", "wo"), (bk,)))
        flags_for(recs)
        p = t % 2
        ks = str(p)
        dst = A["stt"][:, C * p: C * (p + 1)]
        w.op("ve", lambda E, d=dst[:, 0:512]: E.tensor_tensor(d, pO[:, 0:512], A["b2"][:, 0:512], ALU.add),
             reads=("bkO0", "b2"), writes=("stt" + ks,))
        w.op("ve", lambda E, d=dst[:, 512:1024]: E.tensor_tensor(d, pO[:, 512:1024], A["b2"][:, 512:1024], ALU.add),
             reads=("bkO1", "b2"), writes=("stt" + ks,))
        if t == 0 or t == NT - 1:
            rfk = "rf0" if t == 0 else "rfL"
            w.op("ve", lambda E, d=dst, z=A[rfk][:, 0:1]:
                 E.scalar_tensor_tensor(d, A["b2"][:], z, d, ALU.mult, ALU.add),
                 reads=("b2", "consts", "stt" + ks), writes=("stt" + ks,))
        dma("sp", st_sc[128 * t:128 * (t + 1), :], dst, "sst" + ks,
            ("stt" + ks,), ("st_sc",), ckpt=True)

    def wprT_batch(b):
        wkeys = (f"xnTs{2 * b}", f"xnTs{2 * b + 1}", "xnT", f"wprT{b}")
        for jj in range(8):
            fc = 4 * jj + b
            dma("gp", A["wprT"][:, C * fc: C * (fc + 1)],
                wprojT_d[128 * fc:128 * (fc + 1), :],
                f"swp{b}", (), wkeys)

    def xnsc_block(b):
        keys = (f"xnTs{2 * b}", f"xnTs{2 * b + 1}", "xnT")
        for c in range(8):
            dma("sp", xnT_sc[128 * c:128 * (c + 1), 1024 * b: 1024 * (b + 1)],
                A["xnT"][:, L * c + 1024 * b: L * c + 1024 * (b + 1)],
                "sxts", keys, ("xnT_sc",))

    assert NT % 4 == 0 and NS >= 2
    for j in range(2):
        ln_load(j)
    ln_stats(0)
    ln_stats(1)
    for i in range(NS + 2):
        p, qcons = i - 1, i - 2
        if i == NS:
            kr8_prod()
            vx32_prod()
        if i == NS + 1:
            wprT_batch(3)
        chunks = ([("qk", p, 0), ("qk", p, 1), ("qk", p, 2), ("qk", p, 3),
                   ("v", 4 * p), ("v", 4 * p + 1), ("v", 4 * p + 2), ("v", 4 * p + 3)]
                  if 0 <= p < NS else [])
        for j in range(4):
            tln = 4 * i + j
            t = 4 * qcons + j
            if i < NS:
                ln_apply(tln)
                if tln + 2 < NT:
                    ln_load(tln + 2)
            if qcons >= 0:
                scores(t, 0)
                softmax_merge(t, 0)
            for ch in chunks[2 * j: 2 * j + 1]:
                (qk_super(*ch[1:]) if ch[0] == "qk" else v_tile(ch[1]))
            if qcons >= 0:
                scores(t, 1)
                softmax_merge(t, 1)
                if t >= 1:
                    wt_and_o(t - 1, 0)
                    wt_and_o(t - 1, 1)
            for ch in chunks[2 * j + 1: 2 * j + 2]:
                (qk_super(*ch[1:]) if ch[0] == "qk" else v_tile(ch[1]))
            if qcons >= 0 and t >= 1:
                st_tile(t - 1)
            if i < NS:
                xn_transpose(tln)
                if tln + 2 < NT:
                    ln_stats(tln + 2)
        if i >= 2 and (i - 2) % 2 == 0 and (i - 2) // 2 < 3:
            xnsc_block((i - 2) // 2)
        if i == NS:
            xnsc_block(3)
        if i in (3, 5, 7):
            wprT_batch((i - 3) // 2)
        if i == NS:
            for c in range(6):
                dma("gp", A["wfcT"][:, FF * c: FF * (c + 1)],
                    wfcT_d[128 * c:128 * (c + 1), :],
                    "swt", (), ("wfcT6", "wq"))
    wt_and_o(NT - 1, 0)
    wt_and_o(NT - 1, 1)
    st_tile(NT - 1)

    for c in range(6, 8):
        dma("gp", A["wfcT"][:, FF * c: FF * (c + 1)], wfcT_d[128 * c:128 * (c + 1), :],
            "swt3", (), ("wfcT2", "wo"))
    w.barrier(skip_streams=("swt", "swt3", "swp0", "swp1", "swp2", "swp3"))

    NU = L // 256
    hbanks = ((pp0[:, 0:256], "pp0"), (pO[:, 0:256], "bkO0"), (pO[:, 512:768], "bkO1"))

    def c_loads(u):
        up = u % 2
        us = str(up)
        cink = "cin" + us
        for kc in range(8):
            dma("sp", A["xnTc"][:, 2048 * up + 256 * kc: 2048 * up + 256 * (kc + 1)],
                xnT_sc[128 * kc:128 * (kc + 1), 256 * u: 256 * (u + 1)],
                "sc" + us, ("xnT_sc",), (cink,), ckpt=(kc == 0))
        for ii in range(2):
            tt = 2 * u + ii
            dma("sp", A["sti"][:, 2048 * up + C * ii: 2048 * up + C * (ii + 1)],
                st_sc[128 * tt:128 * (tt + 1), :], "sc" + us,
                ("st_sc",), (cink,))
            dma("sp", A["xi"][:, 2048 * up + C * ii: 2048 * up + C * (ii + 1)],
                x_d[128 * tt:128 * (tt + 1), :], "sc" + us,
                (), (cink,))

    def c_hchunk(u, fc):
        up = u % 2
        us = str(up)
        dst, bk = hbanks[fc % 3]
        for kc in range(8):
            wkey = "wfcT6" if kc < 6 else "wfcT2"
            w.op("pe", (lambda E, d=dst,
                        lt=A["wfcT"][:, FF * kc + 128 * fc: FF * kc + 128 * (fc + 1)],
                        r=A["xnTc"][:, 2048 * up + 256 * kc: 2048 * up + 256 * (kc + 1)],
                        s=(kc == 0), p_=(kc == 7):
                        E.matmul(d, lt, r, start=s, stop=p_)),
                 reads=(wkey, "cin" + us), writes=(bk,))
        hr = A["hrl"][:, 256 * (fc % 2): 256 * (fc % 2 + 1)]
        w.op("ae", lambda E, d=hr, i=dst: E.activation(d, i, AF.Relu),
             reads=(bk,), writes=(f"hrl{fc % 2}",))
        w.op("ae", lambda E, d=A["hsb"][:, 256 * fc: 256 * (fc + 1)], i=hr:
             E.activation(d, i, AF.Square),
             reads=(f"hrl{fc % 2}",), writes=("hsb",))

    def c_y_snn(u, ii):
        pshalves = (pS0, pS1) if ii == 0 else (pS2, pp1)
        bks = ("bkS0", "bkS1") if ii == 0 else ("bkS2", "pp1")
        recs = []
        fcs = (sorted(range(32), key=lambda fc: (fc % 4, fc)) if u < 2
               else list(range(32)))
        for fc in fcs:
            lhsT = A["hsb"][:, 256 * fc + 128 * ii: 256 * fc + 128 * (ii + 1)]
            for nh in range(2):
                rhs = A["wprT"][:, C * fc + 512 * nh: C * fc + 512 * (nh + 1)]
                dst = pshalves[nh][:]
                recs.append((bks[nh], (lambda E, s, p_, d=dst, lt=lhsT, r=rhs:
                                       E.matmul(d, lt, r, start=s, stop=p_)),
                             ("hsb", f"wprT{fc % 4}"), (bks[nh],)))
        flags_for(recs)
        ac = A["acc"][:, C * ii: C * (ii + 1)]
        for nh in range(2):
            ah = ac[:, 512 * nh: 512 * (nh + 1)]
            w.op("ve", lambda E, o=ah, y=pshalves[nh]:
                 E.tensor_single_scalar(o, y[:], THS[0], ALU.is_gt),
                 reads=(bks[nh],), writes=(f"acc{ii}",))
            for jj in range(1, 5):
                w.op("ve", lambda E, o=ah, y=pshalves[nh], th=THS[jj]:
                     E.scalar_tensor_tensor(o, y[:], th, o, ALU.is_gt, ALU.add),
                     reads=(bks[nh], f"acc{ii}"), writes=(f"acc{ii}",))

    def c_tail(u, ii):
        up = u % 2
        us = str(up)
        tt = 2 * u + ii
        ac = A["acc"][:, C * ii: C * (ii + 1)]
        g = A["gsb"][:, C * ii: C * (ii + 1)]
        w.op("ae", lambda E, o=g, i_=ac: E.activation(o, i_, AF.Silu, scale=0.2),
             reads=(f"acc{ii}",), writes=(f"gsb{ii}",))
        w.op("ve", lambda E, o=g, a=A["sti"][:, 2048 * up + C * ii: 2048 * up + C * (ii + 1)]:
             E.tensor_tensor(o, a, o, ALU.mult),
             reads=("cin" + us, f"gsb{ii}"), writes=(f"gsb{ii}",))
        ot = A["ot"][:, C * ii: C * (ii + 1)]
        w.op("ve", lambda E, o=ot, a=A["xi"][:, 2048 * up + C * ii: 2048 * up + C * (ii + 1)], b=g:
             E.tensor_tensor(o, a, b, ALU.add),
             reads=("cin" + us, f"gsb{ii}"), writes=(f"ot{ii}",))
        dma("sp", out_d[128 * tt:128 * (tt + 1), :], ot, "so" + str(tt % 2),
            (f"ot{ii}",), ("outd",), ckpt=True)

    c_loads(0)
    for u in range(NU):
        for fc in range(32):
            c_hchunk(u, fc)
            if u > 0 and fc == 3:
                c_tail(u - 1, 0)
            if u > 0 and fc == 7:
                c_tail(u - 1, 1)
            if fc == 11 and u + 1 < NU:
                c_loads(u + 1)
        c_y_snn(u, 0)
        c_y_snn(u, 1)
    c_tail(NU - 1, 0)
    c_tail(NU - 1, 1)

    w.op("sp", lambda E: E.nop(), reads=("outd",))
    final_waits = {"so0": w.cnt.get("so0", 0), "so1": w.cnt.get("so1", 0)}
    w.prog["sp"][-1]["waits"].update(final_waits)

    sems = {}
    import contextlib
    stack = contextlib.ExitStack()
    for name in set(ENGS) | set(w.cnt.keys()) | set(w.stream_eng.keys()):
        sems[name] = stack.enter_context(nc.semaphore("sem_" + name))

    block = stack.enter_context(nc.Block())

    def replay(eng_name):
        def body(E):
            watermark = {}
            for rec in w.prog[eng_name]:
                for s, v in sorted(rec["waits"].items()):
                    if v > 0 and watermark.get(s, -1) < v:
                        E.wait_ge(sems[s], v)
                        watermark[s] = v
                instr = rec["fn"](E)
                if rec["inc"] is not None:
                    instr = instr.then_inc(sems[rec["inc"][0]], rec["inc"][1])
                if rec.get("inc2") is not None:
                    instr.then_inc(sems[rec["inc2"][0]], rec["inc2"][1])
        return body

    block.tensor(replay("pe"))
    block.vector(replay("ve"))
    block.scalar(replay("ae"))
    block.sync(replay("sp"))
    block.gpsimd(replay("gp"))
    stack.close()
    return nc


_CACHE = {}


def kernel(L0_attn_x, L0_attn_w_qkv, L0_attn_w_out, L0_attn_b_out,
           L0_ffn_W_fc, L0_ffn_W_proj):
    x = np.ascontiguousarray(np.asarray(L0_attn_x, np.float32))
    wqkv = np.ascontiguousarray(np.asarray(L0_attn_w_qkv, np.float32))
    wout = np.ascontiguousarray(np.asarray(L0_attn_w_out, np.float32))
    bout = np.ascontiguousarray(np.asarray(L0_attn_b_out, np.float32)).reshape(1, C)
    wfcT = np.ascontiguousarray(np.asarray(L0_ffn_W_fc, np.float32).T)
    wprojT = np.ascontiguousarray(np.asarray(L0_ffn_W_proj, np.float32).T)
    B, L, _ = x.shape
    if "nc" not in _CACHE:
        _CACHE["nc"] = build_program(L)
    nc = _CACHE["nc"]
    in_maps = [{"x": x[b], "w_qkv": wqkv, "w_out": wout, "b_out": bout,
                "w_fcT": wfcT, "w_projT": wprojT} for b in range(B)]
    res = run_bass_kernel_spmd(nc, in_maps, core_ids=list(range(B)))
    return np.stack([res.results[b]["out"] for b in range(B)], axis=0)


# version 25
# speedup vs baseline: 1.1308x; 1.0662x over previous
import numpy as np
import ml_dtypes

import concourse.bass as bass
import concourse.mybir as mybir
from concourse.bass_utils import run_bass_kernel_spmd

F32 = mybir.dt.float32
BF16 = mybir.dt.bfloat16
AF = mybir.ActivationFunctionType
ALU = mybir.AluOpType
AX = mybir.AxisListType
BF = ml_dtypes.bfloat16

C = 1024
NH = 8
DH = 128
FF = 4096
MASKV = 1280.0
SCALE = 1.0 / 32.0
THS = (0.24419428096993973, 0.4639691338428854, 0.6861859295255306,
       0.9059607823984763, 1.0839784132255623)

ENGS = ("pe", "ve", "ae", "sp", "gp")


def _build_masks():
    mA = np.full((128, 132), -MASKV, np.float32)
    mB = np.full((128, 132), -MASKV, np.float32)
    for p in range(128):
        blk = 4 * (p // 4)
        mA[p, blk + 2: blk + 6] = 0.0
        jB = blk if p % 4 < 2 else blk + 4
        mB[p, jB: jB + 4] = 0.0
    return mA.astype(BF), mB.astype(BF)


class Walk:

    def __init__(self):
        self.prog = {e: [] for e in ENGS}
        self.cnt = {}
        self.wmarks = {}
        self.rmarks = {}
        self.pending = {e: [] for e in ENGS}
        self.barrier_waits = {}
        self.stream_eng = {}

    def barrier(self, skip_streams=()):
        vals = {}
        for e in ENGS:
            if self.prog[e]:
                vals[e] = self._mark_val(e)
        for s, eng in self.stream_eng.items():
            if s not in skip_streams:
                vals[s] = self.cnt.get(s, 0)
        for e in ENGS:
            self.barrier_waits[e] = dict(vals)

    def _mark_val(self, eng):
        if self.pending[eng] or (self.prog[eng] and self.prog[eng][-1]["inc"] is None):
            return self._flush(eng)
        return self.cnt.get(eng, 0)


class Walk2(Walk):
    def op(self, eng, fn, reads=(), writes=(), dma=None, ckpt=False, tag=""):
        waits = {}
        for b in reads:
            for e2 in ENGS:
                if b in self.pending[e2]:
                    self._flush(e2)
            for s, v in self.wmarks.get(b, {}).items():
                waits[s] = max(waits.get(s, 0), v)
        for b in writes:
            for e2 in ENGS:
                if b in self.pending[e2] and e2 != eng:
                    self._flush(e2)
            for s, v in self.wmarks.get(b, {}).items():
                if s != eng:
                    waits[s] = max(waits.get(s, 0), v)
            rm = self.rmarks.get(b, {})
            for s in list(rm):
                v = rm[s]
                if v is None:
                    v = self._flush(s)
                    rm[s] = v
                if s != eng:
                    waits[s] = max(waits.get(s, 0), v)
        if dma is not None:
            self.stream_eng[dma] = eng
            if ckpt:
                prev = self.cnt.get(dma, 0)
                if prev:
                    waits[dma] = max(waits.get(dma, 0), prev)
            v = self.cnt.get(dma, 0) + 16
            self.cnt[dma] = v
            inc = (dma, 16)
            for b in writes:
                self.wmarks[b] = {dma: v}
                self.rmarks[b] = {}
            for b in reads:
                self.rmarks.setdefault(b, {})[dma] = v
        else:
            inc = None
            for b in writes:
                self.pending[eng].append(b)
                self.wmarks[b] = {}
                self.rmarks[b] = {}
            for b in reads:
                self.rmarks.setdefault(b, {})[eng] = None
        bw = self.barrier_waits.pop(eng, None)
        if bw:
            for s, v in bw.items():
                waits[s] = max(waits.get(s, 0), v)
        self.prog[eng].append(dict(fn=fn, waits=waits, inc=inc, tag=tag))
        return len(self.prog[eng]) - 1

    def _flush(self, eng):
        ops = self.prog[eng]
        if not ops:
            return self.cnt.get(eng, 0)
        rec = ops[-1]
        if rec["inc"] is None and rec.get("_dma", False) is False:
            v = self.cnt.get(eng, 0) + 1
            self.cnt[eng] = v
            rec["inc"] = (eng, 1)
            for b in self.pending[eng]:
                self.wmarks.setdefault(b, {})[eng] = v
            self.pending[eng] = []
            for b, rm in self.rmarks.items():
                if rm.get(eng, 0) is None:
                    rm[eng] = v
            return v
        if self.pending[eng] or any(rm.get(eng, 0) is None
                                    for rm in self.rmarks.values()):
            if rec["inc"] is not None and rec["inc"][0] != eng:
                extra = rec.get("inc2")
                if extra is None:
                    v = self.cnt.get(eng, 0) + 1
                    self.cnt[eng] = v
                    rec["inc2"] = (eng, 1)
                for b in self.pending[eng]:
                    self.wmarks.setdefault(b, {})[eng] = self.cnt[eng]
                self.pending[eng] = []
                for b, rm in self.rmarks.items():
                    if rm.get(eng, 0) is None:
                        rm[eng] = self.cnt[eng]
                return self.cnt[eng]
        return self.cnt.get(eng, 0)


def build_program(L=4096):
    NT = L // 128
    NS = L // 512
    nc = bass.Bass("TRN2", target_bir_lowering=False)

    x_d = nc.dram_tensor("x", [L, C], F32, kind="ExternalInput").ap()
    wqkv_d = nc.dram_tensor("w_qkv", [C, 3 * C], F32, kind="ExternalInput").ap()
    wout_d = nc.dram_tensor("w_out", [C, C], F32, kind="ExternalInput").ap()
    bout_d = nc.dram_tensor("b_out", [1, C], F32, kind="ExternalInput").ap()
    wfcT_d = nc.dram_tensor("w_fcT", [C, FF], F32, kind="ExternalInput").ap()
    wprojT_d = nc.dram_tensor("w_projT", [FF, C], F32, kind="ExternalInput").ap()
    out_d = nc.dram_tensor("out", [L, C], F32, kind="ExternalOutput").ap()

    xnT_sc = nc.dram_tensor("xnT_sc", [C, L], BF16, kind="Internal").ap()
    st_sc = nc.dram_tensor("st_sc", [L, C], BF16, kind="Internal").ap()

    mA_np, mB_np = _build_masks()
    eye_h = nc.inline_tensor(np.eye(128, dtype=BF), "c_eye")
    ones_h = nc.inline_tensor(np.ones((1, 128), np.float32), "c_ones")
    mA_h = nc.inline_tensor(mA_np, "c_mA")
    mB_h = nc.inline_tensor(mB_np, "c_mB")
    rz0 = np.ones((128, 1), np.float32); rz0[0:2] = 0.0
    rzL = np.ones((128, 1), np.float32); rzL[126:128] = 0.0
    rf0 = np.zeros((128, 1), np.float32); rf0[0:2] = -0.5
    rfL = np.zeros((128, 1), np.float32); rfL[126:128] = -0.5
    zer_h = nc.inline_tensor(np.zeros((128, 1), np.float32), "c_zero")
    eps_h = nc.inline_tensor(np.full((128, 1), 1e-5, np.float32), "c_eps")
    z16_h = nc.inline_tensor(np.zeros((128, 16), BF), "c_z16")
    rz0_h = nc.inline_tensor(rz0, "c_rz0")
    rzL_h = nc.inline_tensor(rzL, "c_rzL")
    rf0_h = nc.inline_tensor(rf0, "c_rf0")
    rfL_h = nc.inline_tensor(rfL, "c_rfL")

    A = {}
    SB_BASE = 16448
    def sb(name, shape, dtype, off):
        A[name] = nc.alloc_sbuf_tensor_at(name, shape, dtype, offset=SB_BASE + off)
        return A[name]

    sb("z16", [128, 16], BF16, -64)
    sb("eye", [128, 128], BF16, 0)
    sb("ones1", [1, 128], F32, 256)
    sb("mA", [128, 132], BF16, 768)
    sb("mB", [128, 132], BF16, 1056)
    sb("rz0", [128, 1], F32, 1344)
    sb("rzL", [128, 1], F32, 1376)
    sb("rf0", [128, 1], F32, 1408)
    sb("rfL", [128, 1], F32, 1440)
    sb("zero", [128, 1], F32, 1472)
    sb("eps", [128, 1], F32, 1504)
    sb("bout", [1, C], F32, 1536)
    sb("b2", [128, C], F32, 5632)
    nc.const_aps.register(F32, 0.0, A["zero"].ap())
    WQ_OFF = 9728
    sb("wq", [128, 8 * 3 * C], BF16, WQ_OFF)
    sb("wo", [128, 8 * C], BF16, WQ_OFF + 49152)
    sb("wfcT", [128, 8 * FF], BF16, WQ_OFF)
    XNT_OFF = WQ_OFF + 65536
    sb("xnT", [128, 8 * L], BF16, XNT_OFF)
    sb("wprT", [128, 32 * C], BF16, XNT_OFF)
    AR = XNT_OFF + 16 * L
    sb("qr", [128, 2 * NH * 512], BF16, AR)
    sb("kr", [128, 2 * NH * 512], BF16, AR + 16384)
    sb("vr", [128, 5 * C], BF16, AR + 32768)
    sb("xt", [128, 2 * C], F32, AR + 43008)
    sb("vx32", [2, C], BF16, AR + 43008)
    sb("kr8", [128, 16], BF16, AR + 45056)
    sb("xnb", [128, 2 * C], BF16, AR + 51200)
    sb("expA", [128, 528], BF16, AR + 55296)
    sb("expB", [128, 528], BF16, AR + 56352)
    sb("wb", [128, 4 * 528], BF16, AR + 57408)
    sb("stt", [128, 2 * C], BF16, AR + 61632)
    sb("wtm", [128, 1024], BF16, AR + 65728)
    sb("osb", [128, 1024], BF16, AR + 67776)
    sb("wte", [4, 1024], BF16, AR + 69824)
    sb("astat", [128, 16], F32, AR + 71872)
    sb("rsb", [128, 32], F32, AR + 71936)
    assert SB_BASE + AR + 72064 <= 229368, f"sbuf overflow B: {SB_BASE + AR + 72064}"
    sb("xnTc", [128, 4096], BF16, AR)
    sb("hsb", [128, 8192], BF16, AR + 8192)
    sb("hrl", [128, 512], BF16, AR + 24576)
    sb("acc", [128, 2048], F32, AR + 25600)
    sb("gsb", [128, 2048], BF16, AR + 33792)
    sb("sti", [128, 4096], BF16, AR + 37888)
    sb("xi", [128, 4096], F32, AR + 46080)
    sb("ot", [128, 2048], F32, AR + 62464)
    assert SB_BASE + AR + 70656 <= 229376

    pp0 = nc.alloc_psum_tensor("pp0", [128, 512], F32)
    pp1 = nc.alloc_psum_tensor("pp1", [128, 512], F32)
    pS0 = nc.alloc_psum_tensor("pS0", [128, 512], F32)
    pS1 = nc.alloc_psum_tensor("pS1", [128, 512], F32)
    pS2 = nc.alloc_psum_tensor("pS2", [128, 512], F32)
    pT = nc.alloc_psum_tensor("pT", [128, 1024], BF16)
    pO = nc.alloc_psum_tensor("pO", [128, 1024], F32)

    w = Walk2()
    rotc = [0]

    def rot():
        k = rotc[0] & 1
        rotc[0] += 1
        return (pp0, "pp0") if k == 0 else (pp1, "pp1")

    def dma(eng, dst, src, stream, reads, writes, ckpt=False, transpose=False):
        if transpose:
            w.op(eng, lambda E, d=dst, s=src: E.dma_start_transpose(d, s),
                 reads=reads, writes=writes, dma=stream, ckpt=ckpt)
        else:
            w.op(eng, lambda E, d=dst, s=src: E.dma_start(d, s),
                 reads=reads, writes=writes, dma=stream, ckpt=ckpt)

    def flags_for(recs):
        first = {}
        last = {}
        for i, (bank, _fn, _r, _w) in enumerate(recs):
            if bank not in first:
                first[bank] = i
            last[bank] = i
        for i, (bank, fn, rd, wr) in enumerate(recs):
            st = (i == first[bank])
            sp_ = (i == last[bank])
            w.op("pe", (lambda E, f=fn, a=st, b=sp_: f(E, a, b)),
                 reads=rd, writes=wr)

    for name, hdl in (("eye", eye_h), ("ones1", ones_h), ("mA", mA_h),
                      ("mB", mB_h), ("rz0", rz0_h), ("rzL", rzL_h),
                      ("rf0", rf0_h), ("rfL", rfL_h), ("zero", zer_h),
                      ("eps", eps_h), ("z16", z16_h)):
        dma("sp", A[name][:], hdl.ap(), "smc", (), ("consts", "bout"))
    dma("sp", A["bout"][:], bout_d, "smc", (), ("consts", "bout"))
    for _e in ENGS:
        w.barrier_waits[_e] = {"smc": w.cnt["smc"]}
    for kc in range(8):
        dma("gp", A["wq"][:, 3 * C * kc: 3 * C * (kc + 1)],
            wqkv_d[128 * kc:128 * (kc + 1), :], "swq", (), ("wq",))
    for kc in range(8):
        dma("gp", A["wo"][:, C * kc: C * (kc + 1)],
            wout_d[128 * kc:128 * (kc + 1), :], "swo", (), ("wo",))
    w.op("pe", lambda E: E.matmul(pp0[:], A["ones1"][:], A["bout"][:, 0:512],
                                  start=True, stop=True),
         reads=("bout", "consts"), writes=("pp0",))
    w.op("pe", lambda E: E.matmul(pp1[:], A["ones1"][:], A["bout"][:, 512:1024],
                                  start=True, stop=True),
         reads=("bout", "consts"), writes=("pp1",))
    w.op("ae", lambda E: E.activation(A["b2"][:, 0:512], pp0[:], AF.Copy, scale=2.0),
         reads=("pp0",), writes=("b2",))
    w.op("ae", lambda E: E.activation(A["b2"][:, 512:1024], pp1[:], AF.Copy, scale=2.0),
         reads=("pp1",), writes=("b2",))
    w.op("ve", lambda E: E.tensor_copy(
        A["kr"][:, 0: NH * 512].rearrange("p (h c) -> p h c", h=NH)[:, :, 0:2],
        A["z16"][:].rearrange("p (h c) -> p h c", h=NH)),
         reads=("consts",), writes=("k0",))

    inv_c = 1.0 / C

    def ln_load(t):
        p = t % 2
        dma("sp", A["xt"][:, C * p: C * (p + 1)], x_d[128 * t:128 * (t + 1), :],
            "sx" + str(p), (), (f"xt{p}",), ckpt=True)

    def ln_stats(t):
        p = t % 2
        xt = A["xt"][:, C * p: C * (p + 1)]
        xn = A["xnb"][:, C * p: C * (p + 1)]
        st = A["astat"][:, 8 * p: 8 * p + 8]
        w.op("ve", lambda E, o=st, i=xt: E.tensor_reduce(o[:, 0:1], i, AX.X, ALU.add),
             reads=(f"xt{p}",), writes=(f"astat{p}",))
        w.op("ae", lambda E, i=xt, s=st, d=xn: E.activation(d, i, AF.Square,
                                                            accum_out=s[:, 1:2]),
             reads=(f"xt{p}",), writes=(f"astat{p}", f"xnb{p}"))
        w.op("ve", lambda E, s=st: E.tensor_scalar(s[:, 2:3], s[:, 0:1], inv_c, None, ALU.mult),
             reads=(f"astat{p}",), writes=(f"astat{p}",))
        w.op("ve", lambda E, s=st: E.tensor_tensor(s[:, 3:4], s[:, 2:3], s[:, 2:3], ALU.mult),
             reads=(f"astat{p}",), writes=(f"astat{p}",))
        w.op("ve", lambda E, s=st: E.tensor_scalar(s[:, 3:4], s[:, 3:4], -1e-5, None, ALU.add),
             reads=(f"astat{p}",), writes=(f"astat{p}",))
        w.op("ve", lambda E, s=st: E.scalar_tensor_tensor(
            s[:, 4:5], s[:, 1:2], inv_c, s[:, 3:4], ALU.mult, ALU.subtract),
             reads=(f"astat{p}",), writes=(f"astat{p}",))
        w.op("ae", lambda E, s=st: E.activation(s[:, 5:6], s[:, 4:5], AF.Ln),
             reads=(f"astat{p}",), writes=(f"astat{p}",))
        w.op("ae", lambda E, s=st: E.activation(s[:, 6:7], s[:, 5:6], AF.Exp,
                                                scale=-0.5),
             reads=(f"astat{p}",), writes=(f"astat{p}",))
        w.op("ve", lambda E, s=st: E.scalar_tensor_tensor(
            s[:, 7:8], s[:, 2:3], -1.0, s[:, 6:7], ALU.mult, ALU.mult),
             reads=(f"astat{p}",), writes=(f"astat{p}",))

    def ln_apply(t):
        p = t % 2
        xt = A["xt"][:, C * p: C * (p + 1)]
        xn = A["xnb"][:, C * p: C * (p + 1)]
        st = A["astat"][:, 8 * p: 8 * p + 8]
        w.op("ae", lambda E, i=xt, s=st, d=xn: E.activation(
            d, i, AF.Identity, scale=s[:, 6:7], bias=s[:, 7:8]),
             reads=(f"xt{p}", f"astat{p}"), writes=(f"xnb{p}",))

    def xn_transpose(t):
        p4 = t % 2
        bkey = "pT"
        bb = pT[:]
        for c in range(8):
            w.op("pe", lambda E, d=bb[:, 128 * c: 128 * (c + 1)],
                 i=A["xnb"][:, C * p4 + 128 * c: C * p4 + 128 * (c + 1)]:
                 E.transpose(d, i, A["eye"][:]),
                 reads=(f"xnb{p4}", "consts"), writes=(bkey,))
        dst = A["xnT"][:].rearrange("p (c l) -> p c l", c=8)[
            :, :, 128 * t: 128 * (t + 1)]
        src = bb.rearrange("p (c l) -> p c l", c=8)
        if t % 2 == 0:
            w.op("ve", lambda E, d=dst, s=src: E.tensor_copy(d, s),
                 reads=(bkey,), writes=(f"xnTs{t // 4}", "xnT"))
        else:
            w.op("ae", lambda E, d=dst, s=src: E.activation(d, s, AF.Copy),
                 reads=(bkey,), writes=(f"xnTs{t // 4}", "xnT"))

    def qk_super(s, part):
        isq = part < 2
        hs = 4 * (part % 2)
        sl = s % 2
        if isq:
            tok0, N, doff = 512 * s, 512, 0
        elif s > 0:
            tok0, N, doff = 512 * s - 2, 512, 0
        else:
            tok0, N, doff = 0, 510, 2
        qkoff = 0 if isq else C
        dst_sb = A["qr"] if isq else A["kr"]
        key = (f"q{sl}" if isq else f"k{sl}")
        rds = ("wq", f"xnTs{s}") + ((f"xnTs{s - 1}",) if (not isq and s > 0) else ())
        for h in range(hs, hs + 4):
            bank, bkey = rot()
            for kc in range(8):
                w.op("pe", (lambda E, d=bank[:, doff: doff + N],
                            lt=A["wq"][:, 3 * C * kc + qkoff + 128 * h:
                                       3 * C * kc + qkoff + 128 * (h + 1)],
                            r=A["xnT"][:, L * kc + tok0: L * kc + tok0 + N],
                            st_=(kc == 0), sp_=(kc == 7):
                            E.matmul(d, lt, r, start=st_, stop=sp_)),
                     reads=rds, writes=(bkey,))
            w.op("ae", lambda E, d=dst_sb[:, 4096 * sl + 512 * h + doff:
                                          4096 * sl + 512 * h + doff + N],
                 s_=bank[:, doff: doff + N]: E.activation(d, s_, AF.Copy),
                 reads=(bkey,), writes=(key,))

    def v_tile(u):
        vstart = 0 if u == 0 else 128 * u - 2
        sup = vstart // 512
        rds = ("wq", f"xnTs{u // 4}") + ((f"xnTs{sup}",) if sup != u // 4 else ())
        for nh in range(2):
            bank, bkey = rot()
            for kc in range(8):
                w.op("pe", (lambda E, d=bank[:],
                            lt=A["xnT"][:, L * kc + vstart: L * kc + vstart + 128],
                            r=A["wq"][:, 3 * C * kc + 2 * C + 512 * nh:
                                      3 * C * kc + 2 * C + 512 * (nh + 1)],
                            st_=(kc == 0), sp_=(kc == 7):
                            E.matmul(d, lt, r, start=st_, stop=sp_)),
                     reads=rds, writes=(bkey,))
            w.op("ae", lambda E, d=A["vr"][:, C * (u % 5) + 512 * nh:
                                           C * (u % 5) + 512 * (nh + 1)],
                 s_=bank[:]: E.activation(d, s_, AF.Copy),
                 reads=(bkey,), writes=(f"v{u % 5}",))

    def kr8_prod():
        bank, bkey = rot()
        recs = []
        for h in range(8):
            for kc in range(8):
                recs.append((bkey, (lambda E, st_, sp_,
                                    d=bank[:, 2 * h: 2 * h + 2],
                                    lt=A["wq"][:, 3 * C * kc + C + 128 * h:
                                               3 * C * kc + C + 128 * (h + 1)],
                                    r=A["xnT"][:, L * kc + L - 2: L * kc + L]:
                                    E.matmul(d, lt, r, start=st_, stop=sp_)),
                             ("wq", f"xnTs{NS - 1}"), (bkey,)))
        flags_for(recs)
        w.op("ve", lambda E, d=A["kr8"][:], s_=bank[:, 0:16]: E.tensor_copy(d, s_),
             reads=(bkey,), writes=("kr8",))

    def vx32_prod():
        for nh in range(2):
            bank, bkey = rot()
            for kc in range(8):
                w.op("pe", (lambda E, d=bank[0:2, :],
                            lt=A["xnT"][:, L * kc + L - 2: L * kc + L],
                            r=A["wq"][:, 3 * C * kc + 2 * C + 512 * nh:
                                      3 * C * kc + 2 * C + 512 * (nh + 1)],
                            st_=(kc == 0), sp_=(kc == 7):
                            E.matmul(d, lt, r, start=st_, stop=sp_)),
                     reads=("wq", f"xnTs{NS - 1}"), writes=(bkey,))
            w.op("ve", lambda E, d=A["vx32"][0:2, 512 * nh: 512 * (nh + 1)],
                 s_=bank[0:2, :]: E.tensor_copy(d, s_),
                 reads=(bkey,), writes=("vx32",))

    SBANKS = ((pS0, "bkS0"), (pS1, "bkS1"), (pS2, "bkS2"))

    def scores(t, g):
        s_sup, j = t // 4, t % 4
        ksl = (s_sup % 2)
        recs = []
        for hh in range(4):
            h = 4 * g + hh
            lhsT = A["qr"][:, 4096 * ksl + 512 * h + 128 * j:
                           4096 * ksl + 512 * h + 128 * j + 128]
            for seti, mkey in ((0, "mA"), (1, "mB")):
                r = 4 * seti + hh
                ps, bk = SBANKS[r // 3]
                reg0 = 132 * (r % 3)
                segs = []
                if j < 3:
                    segs.append((A["kr"][:, 4096 * ksl + 512 * h + 128 * j:
                                         4096 * ksl + 512 * h + 128 * j + 132],
                                 0, 132, f"k{ksl}"))
                else:
                    segs.append((A["kr"][:, 4096 * ksl + 512 * h + 384:
                                         4096 * ksl + 512 * h + 512],
                                 0, 128, f"k{ksl}"))
                    if t < NT - 1:
                        segs.append((A["kr"][:, 4096 * ((s_sup + 1) % 2) + 512 * h:
                                             4096 * ((s_sup + 1) % 2) + 512 * h + 4],
                                     128, 132, f"k{(s_sup + 1) % 2}"))
                    else:
                        segs.append((A["kr8"][:, 2 * h: 2 * h + 2],
                                     128, 130, "kr8"))
                recs.append((bk, (lambda E, st_, sp_, d=ps[:, reg0: reg0 + 132],
                                  r=A[mkey][:]:
                                  E.matmul(d, A["eye"][:], r, start=st_, stop=sp_)),
                             ("consts",), (bk,)))
                for rhs, c0, c1, kkey in segs:
                    dst = ps[:, reg0 + c0: reg0 + c1]
                    recs.append((bk, (lambda E, st_, sp_, d=dst, lt=lhsT, r=rhs:
                                      E.matmul(d, lt, r, start=st_, stop=sp_)),
                                 (f"q{ksl}", kkey), (bk,)))
        flags_for(recs)

    def softmax_merge(t, g):
        s = g
        ss = str(s)
        s2 = 2 * (t % 2) + g
        expA = A["expA"][:, 0:528]
        expB = A["expB"][:, 0:528]
        rs = A["rsb"]
        wkey = f"wb{t % 2}{g}"
        w.op("ae", lambda E, o=expA[:, 0:396], p_=pS0:
             E.activation(o, p_[:, 0:396], AF.Exp, scale=SCALE),
             reads=("bkS0",), writes=("expA",))
        w.op("ae", lambda E, o=expA[:, 396:528], p_=pS1:
             E.activation(o, p_[:, 0:132], AF.Exp, scale=SCALE),
             reads=("bkS1",), writes=("expA",))
        w.op("ae", lambda E, o=expB[:, 0:264], p_=pS1:
             E.activation(o, p_[:, 132:396], AF.Exp, scale=SCALE),
             reads=("bkS1",), writes=("expB",))
        w.op("ae", lambda E, o=expB[:, 264:528], p_=pS2:
             E.activation(o, p_[:, 0:264], AF.Exp, scale=SCALE),
             reads=("bkS2",), writes=("expB",))
        w.op("ve", lambda E, i=expA, o=rs[:, 16 * s: 16 * s + 4]:
             E.tensor_reduce(o, i.rearrange("p (h j) -> p h j", h=4), AX.X, ALU.add),
             reads=("expA",), writes=("rs" + ss,))
        w.op("ve", lambda E, i=expB, o=rs[:, 16 * s + 4: 16 * s + 8]:
             E.tensor_reduce(o, i.rearrange("p (h j) -> p h j", h=4), AX.X, ALU.add),
             reads=("expB",), writes=("rs" + ss,))
        w.op("ve", lambda E, o=rs[:, 16 * s + 8: 16 * s + 16],
             i=rs[:, 16 * s: 16 * s + 8]: E.reciprocal(o, i),
             reads=("rs" + ss,), writes=("rs" + ss,))
        if t == 0 or t == NT - 1:
            rzk = "rz0" if t == 0 else "rzL"
            w.op("ve", lambda E, o=rs[:, 16 * s + 12: 16 * s + 16], z=A[rzk][:, 0:1]:
                 E.tensor_scalar(o, o, z, None, ALU.mult),
                 reads=("rs" + ss, "consts"), writes=("rs" + ss,))
        for hh in range(4):
            wdst = A["wb"][:, 528 * s2 + 132 * hh: 528 * s2 + 132 * (hh + 1)]
            w.op("ve", lambda E, o=wdst, i=expA[:, 132 * hh: 132 * (hh + 1)],
                 r=rs[:, 16 * s + 8 + hh: 16 * s + 9 + hh]:
                 E.tensor_scalar(o, i, r, None, ALU.mult),
                 reads=("expA", "rs" + ss), writes=(wkey,))
            w.op("ve", lambda E, o=wdst, i=expB[:, 132 * hh: 132 * (hh + 1)],
                 r=rs[:, 16 * s + 12 + hh: 16 * s + 13 + hh]:
                 E.scalar_tensor_tensor(o, i, r, o, ALU.mult, ALU.add),
                 reads=("expB", "rs" + ss, wkey), writes=(wkey,))

    def wt_and_o(t, g):
        s = g
        s2 = 2 * (t % 2) + g
        ss = f"wb{t % 2}{g}"
        shift = 2 if t == 0 else 0
        bkey = "pT"
        bb = pT[:]
        for hh in range(4):
            w.op("pe", lambda E, d=bb[:, 128 * hh: 128 * (hh + 1)],
                 i=A["wb"][:, 528 * s2 + 132 * hh + shift: 528 * s2 + 132 * hh + shift + 128]:
                 E.transpose(d, i, A["eye"][:]),
                 reads=(ss, "consts"), writes=(bkey,))
        for hh in range(4):
            w.op("pe", lambda E, d=bb[0:4, 512 + 128 * hh: 512 + 128 * (hh + 1)],
                 i=A["wb"][:, 528 * s2 + 132 * hh + 128: 528 * s2 + 132 * (hh + 1)]:
                 E.transpose(d, i, A["eye"][:]),
                 reads=(ss, "consts"), writes=(bkey,))
        w.op("ve", lambda E, d=A["wtm"][:, 512 * g: 512 * (g + 1)], s_=bb[:, 0:512]:
             E.tensor_copy(d, s_),
             reads=(bkey,), writes=("wtm",))
        w.op("ve", lambda E, d=A["wte"][0:4, 512 * g: 512 * (g + 1)],
             s_=bb[0:4, 512:1024]: E.tensor_copy(d, s_),
             reads=(bkey,), writes=("wte",))
        if t == 0:
            w.op("ve", lambda E, d=A["wte"][0:2, 512 * g: 512 * (g + 1)],
                 i=A["wtm"][0:2, 512 * g: 512 * (g + 1)]:
                 E.tensor_scalar(d, i, 0.0, None, ALU.mult),
                 reads=("wte", "wtm"), writes=("wte",))
        obk = "bkO" + str(g)
        orecs = []
        for hh in range(4):
            h = 4 * g + hh
            dst = pO[:, 128 * h: 128 * (h + 1)]
            orecs.append((obk, (lambda E, st_, sp_, d=dst,
                                lt=A["vr"][:, C * (t % 5) + 128 * h:
                                           C * (t % 5) + 128 * (h + 1)],
                                r=A["wtm"][:, 128 * h: 128 * (h + 1)]:
                                E.matmul(d, lt, r, start=st_, stop=sp_)),
                          (f"v{t % 5}", "wtm"), (obk,)))
            if t == 0:
                lt2 = A["vr"][0:4, C * 1 + 128 * h: C * 1 + 128 * h + 128]
                r2 = A["wte"][0:4, 128 * h: 128 * (h + 1)]
                vkey = "v1"
            elif t == NT - 1:
                lt2 = A["vx32"][0:2, 128 * h: 128 * h + 128]
                r2 = A["wte"][0:2, 128 * h: 128 * (h + 1)]
                vkey = "vx32"
            else:
                lt2 = A["vr"][0:4, C * ((t + 1) % 5) + 128 * h:
                              C * ((t + 1) % 5) + 128 * h + 128]
                r2 = A["wte"][0:4, 128 * h: 128 * (h + 1)]
                vkey = f"v{(t + 1) % 5}"
            orecs.append((obk, (lambda E, st_, sp_, d=dst, lt=lt2, r=r2:
                                E.matmul(d, lt, r, start=st_, stop=sp_)),
                          (vkey, "wte"), (obk,)))
        flags_for(orecs)
        w.op("ve", lambda E, d=A["osb"][:, 512 * g: 512 * (g + 1)],
             s_=pO[:, 512 * g: 512 * (g + 1)]: E.tensor_copy(d, s_),
             reads=(obk,), writes=("osb",))

    def st_tile(t):
        recs = []
        for nh in range(2):
            bk = "bkO" + str(nh)
            dstp = pO[:, 512 * nh: 512 * (nh + 1)]
            for cin in range(8):
                recs.append((bk, (lambda E, st_, sp_, d=dstp,
                                  lt=A["osb"][:, 128 * cin: 128 * (cin + 1)],
                                  r=A["wo"][:, C * cin + 512 * nh:
                                            C * cin + 512 * (nh + 1)]:
                                  E.matmul(d, lt, r, start=st_, stop=sp_)),
                             ("osb", "wo"), (bk,)))
        flags_for(recs)
        p = t % 2
        ks = str(p)
        dst = A["stt"][:, C * p: C * (p + 1)]
        w.op("ve", lambda E, d=dst[:, 0:512]: E.tensor_tensor(d, pO[:, 0:512], A["b2"][:, 0:512], ALU.add),
             reads=("bkO0", "b2"), writes=("stt" + ks,))
        w.op("ve", lambda E, d=dst[:, 512:1024]: E.tensor_tensor(d, pO[:, 512:1024], A["b2"][:, 512:1024], ALU.add),
             reads=("bkO1", "b2"), writes=("stt" + ks,))
        if t == 0 or t == NT - 1:
            rfk = "rf0" if t == 0 else "rfL"
            w.op("ve", lambda E, d=dst, z=A[rfk][:, 0:1]:
                 E.scalar_tensor_tensor(d, A["b2"][:], z, d, ALU.mult, ALU.add),
                 reads=("b2", "consts", "stt" + ks), writes=("stt" + ks,))
        dma("sp", st_sc[128 * t:128 * (t + 1), :], dst, "sst" + ks,
            ("stt" + ks,), ("st_sc",), ckpt=True)

    def wprT_batch(b):
        wkeys = (f"xnTs{2 * b}", f"xnTs{2 * b + 1}", "xnT", f"wprT{b}")
        for jj in range(8):
            fc = 4 * jj + b
            dma("gp", A["wprT"][:, C * fc: C * (fc + 1)],
                wprojT_d[128 * fc:128 * (fc + 1), :],
                f"swp{b}", (), wkeys)

    def xnsc_block(b):
        keys = (f"xnTs{2 * b}", f"xnTs{2 * b + 1}", "xnT")
        for c in range(8):
            dma("sp", xnT_sc[128 * c:128 * (c + 1), 1024 * b: 1024 * (b + 1)],
                A["xnT"][:, L * c + 1024 * b: L * c + 1024 * (b + 1)],
                "sxts", keys, ("xnT_sc",))

    assert NT % 4 == 0 and NS >= 2
    for j in range(2):
        ln_load(j)
    ln_stats(0)
    ln_stats(1)
    for i in range(NS + 2):
        p, qcons = i - 1, i - 2
        if i == NS:
            kr8_prod()
            vx32_prod()
        if i == NS + 1:
            wprT_batch(3)
        chunks = ([("qk", p, 0), ("qk", p, 1), ("qk", p, 2), ("qk", p, 3),
                   ("v", 4 * p), ("v", 4 * p + 1), ("v", 4 * p + 2), ("v", 4 * p + 3)]
                  if 0 <= p < NS else [])
        for j in range(4):
            tln = 4 * i + j
            t = 4 * qcons + j
            if i < NS:
                ln_apply(tln)
                if tln + 2 < NT:
                    ln_load(tln + 2)
            if qcons >= 0:
                scores(t, 0)
                softmax_merge(t, 0)
            for ch in chunks[2 * j: 2 * j + 1]:
                (qk_super(*ch[1:]) if ch[0] == "qk" else v_tile(ch[1]))
            if qcons >= 0:
                scores(t, 1)
                if t >= 1:
                    wt_and_o(t - 1, 0)
                    wt_and_o(t - 1, 1)
                softmax_merge(t, 1)
            for ch in chunks[2 * j + 1: 2 * j + 2]:
                (qk_super(*ch[1:]) if ch[0] == "qk" else v_tile(ch[1]))
            if qcons >= 0 and t >= 1:
                st_tile(t - 1)
            if i < NS:
                xn_transpose(tln)
                if tln + 2 < NT:
                    ln_stats(tln + 2)
        if i >= 2 and (i - 2) % 2 == 0 and (i - 2) // 2 < 3:
            xnsc_block((i - 2) // 2)
        if i == NS:
            xnsc_block(3)
        if i in (3, 5, 7):
            wprT_batch((i - 3) // 2)
        if i == NS:
            for c in range(6):
                dma("gp", A["wfcT"][:, FF * c: FF * (c + 1)],
                    wfcT_d[128 * c:128 * (c + 1), :],
                    "swt", (), ("wfcT6", "wq"))
    wt_and_o(NT - 1, 0)
    wt_and_o(NT - 1, 1)
    st_tile(NT - 1)

    for c in range(6, 8):
        dma("gp", A["wfcT"][:, FF * c: FF * (c + 1)], wfcT_d[128 * c:128 * (c + 1), :],
            "swt3", (), ("wfcT2", "wo"))
    w.barrier(skip_streams=("swt", "swt3", "swp0", "swp1", "swp2", "swp3"))

    NU = L // 256
    hbanks = ((pp0[:, 0:256], "pp0"), (pO[:, 0:256], "bkO0"), (pO[:, 512:768], "bkO1"))

    def c_loads(u):
        up = u % 2
        us = str(up)
        cink = "cin" + us
        for kc in range(8):
            dma("sp", A["xnTc"][:, 2048 * up + 256 * kc: 2048 * up + 256 * (kc + 1)],
                xnT_sc[128 * kc:128 * (kc + 1), 256 * u: 256 * (u + 1)],
                "sc" + us, ("xnT_sc",), (cink,), ckpt=(kc == 0))
        for ii in range(2):
            tt = 2 * u + ii
            dma("sp", A["sti"][:, 2048 * up + C * ii: 2048 * up + C * (ii + 1)],
                st_sc[128 * tt:128 * (tt + 1), :], "sc" + us,
                ("st_sc",), (cink,))
            dma("sp", A["xi"][:, 2048 * up + C * ii: 2048 * up + C * (ii + 1)],
                x_d[128 * tt:128 * (tt + 1), :], "sc" + us,
                (), (cink,))

    def c_hchunk(u, fc):
        up = u % 2
        us = str(up)
        dst, bk = hbanks[fc % 3]
        for kc in range(8):
            wkey = "wfcT6" if kc < 6 else "wfcT2"
            w.op("pe", (lambda E, d=dst,
                        lt=A["wfcT"][:, FF * kc + 128 * fc: FF * kc + 128 * (fc + 1)],
                        r=A["xnTc"][:, 2048 * up + 256 * kc: 2048 * up + 256 * (kc + 1)],
                        s=(kc == 0), p_=(kc == 7):
                        E.matmul(d, lt, r, start=s, stop=p_)),
                 reads=(wkey, "cin" + us), writes=(bk,))
        hr = A["hrl"][:, 256 * (fc % 2): 256 * (fc % 2 + 1)]
        w.op("ae", lambda E, d=hr, i=dst: E.activation(d, i, AF.Relu),
             reads=(bk,), writes=(f"hrl{fc % 2}",))
        w.op("ae", lambda E, d=A["hsb"][:, 256 * fc: 256 * (fc + 1)], i=hr:
             E.activation(d, i, AF.Square),
             reads=(f"hrl{fc % 2}",), writes=("hsb",))

    def c_y_snn(u, ii):
        pshalves = (pS0, pS1) if ii == 0 else (pS2, pp1)
        bks = ("bkS0", "bkS1") if ii == 0 else ("bkS2", "pp1")
        recs = []
        fcs = (sorted(range(32), key=lambda fc: (fc % 4, fc)) if u < 2
               else list(range(32)))
        for fc in fcs:
            lhsT = A["hsb"][:, 256 * fc + 128 * ii: 256 * fc + 128 * (ii + 1)]
            for nh in range(2):
                rhs = A["wprT"][:, C * fc + 512 * nh: C * fc + 512 * (nh + 1)]
                dst = pshalves[nh][:]
                recs.append((bks[nh], (lambda E, s, p_, d=dst, lt=lhsT, r=rhs:
                                       E.matmul(d, lt, r, start=s, stop=p_)),
                             ("hsb", f"wprT{fc % 4}"), (bks[nh],)))
        flags_for(recs)
        ac = A["acc"][:, C * ii: C * (ii + 1)]
        for nh in range(2):
            ah = ac[:, 512 * nh: 512 * (nh + 1)]
            w.op("ve", lambda E, o=ah, y=pshalves[nh]:
                 E.tensor_single_scalar(o, y[:], THS[0], ALU.is_gt),
                 reads=(bks[nh],), writes=(f"acc{ii}",))
            for jj in range(1, 5):
                w.op("ve", lambda E, o=ah, y=pshalves[nh], th=THS[jj]:
                     E.scalar_tensor_tensor(o, y[:], th, o, ALU.is_gt, ALU.add),
                     reads=(bks[nh], f"acc{ii}"), writes=(f"acc{ii}",))

    def c_tail(u, ii):
        up = u % 2
        us = str(up)
        tt = 2 * u + ii
        ac = A["acc"][:, C * ii: C * (ii + 1)]
        g = A["gsb"][:, C * ii: C * (ii + 1)]
        w.op("ae", lambda E, o=g, i_=ac: E.activation(o, i_, AF.Silu, scale=0.2),
             reads=(f"acc{ii}",), writes=(f"gsb{ii}",))
        w.op("ve", lambda E, o=g, a=A["sti"][:, 2048 * up + C * ii: 2048 * up + C * (ii + 1)]:
             E.tensor_tensor(o, a, o, ALU.mult),
             reads=("cin" + us, f"gsb{ii}"), writes=(f"gsb{ii}",))
        ot = A["ot"][:, C * ii: C * (ii + 1)]
        w.op("ve", lambda E, o=ot, a=A["xi"][:, 2048 * up + C * ii: 2048 * up + C * (ii + 1)], b=g:
             E.tensor_tensor(o, a, b, ALU.add),
             reads=("cin" + us, f"gsb{ii}"), writes=(f"ot{ii}",))
        dma("sp", out_d[128 * tt:128 * (tt + 1), :], ot, "so" + str(tt % 2),
            (f"ot{ii}",), ("outd",), ckpt=True)

    c_loads(0)
    for u in range(NU):
        for fc in range(32):
            c_hchunk(u, fc)
            if u > 0 and fc == 3:
                c_tail(u - 1, 0)
            if u > 0 and fc == 7:
                c_tail(u - 1, 1)
            if fc == 11 and u + 1 < NU:
                c_loads(u + 1)
        c_y_snn(u, 0)
        c_y_snn(u, 1)
    c_tail(NU - 1, 0)
    c_tail(NU - 1, 1)

    w.op("sp", lambda E: E.nop(), reads=("outd",))
    final_waits = {"so0": w.cnt.get("so0", 0), "so1": w.cnt.get("so1", 0)}
    w.prog["sp"][-1]["waits"].update(final_waits)

    sems = {}
    import contextlib
    stack = contextlib.ExitStack()
    for name in set(ENGS) | set(w.cnt.keys()) | set(w.stream_eng.keys()):
        sems[name] = stack.enter_context(nc.semaphore("sem_" + name))

    block = stack.enter_context(nc.Block())

    def replay(eng_name):
        def body(E):
            watermark = {}
            for rec in w.prog[eng_name]:
                for s, v in sorted(rec["waits"].items()):
                    if v > 0 and watermark.get(s, -1) < v:
                        E.wait_ge(sems[s], v)
                        watermark[s] = v
                instr = rec["fn"](E)
                if rec["inc"] is not None:
                    instr = instr.then_inc(sems[rec["inc"][0]], rec["inc"][1])
                if rec.get("inc2") is not None:
                    instr.then_inc(sems[rec["inc2"][0]], rec["inc2"][1])
        return body

    block.tensor(replay("pe"))
    block.vector(replay("ve"))
    block.scalar(replay("ae"))
    block.sync(replay("sp"))
    block.gpsimd(replay("gp"))
    stack.close()
    return nc


_CACHE = {}


def kernel(L0_attn_x, L0_attn_w_qkv, L0_attn_w_out, L0_attn_b_out,
           L0_ffn_W_fc, L0_ffn_W_proj):
    x = np.ascontiguousarray(np.asarray(L0_attn_x, np.float32))
    wqkv = np.ascontiguousarray(np.asarray(L0_attn_w_qkv, np.float32))
    wout = np.ascontiguousarray(np.asarray(L0_attn_w_out, np.float32))
    bout = np.ascontiguousarray(np.asarray(L0_attn_b_out, np.float32)).reshape(1, C)
    wfcT = np.ascontiguousarray(np.asarray(L0_ffn_W_fc, np.float32).T)
    wprojT = np.ascontiguousarray(np.asarray(L0_ffn_W_proj, np.float32).T)
    B, L, _ = x.shape
    if "nc" not in _CACHE:
        _CACHE["nc"] = build_program(L)
    nc = _CACHE["nc"]
    in_maps = [{"x": x[b], "w_qkv": wqkv, "w_out": wout, "b_out": bout,
                "w_fcT": wfcT, "w_projT": wprojT} for b in range(B)]
    res = run_bass_kernel_spmd(nc, in_maps, core_ids=list(range(B)))
    return np.stack([res.results[b]["out"] for b in range(B)], axis=0)


# version 26
# speedup vs baseline: 1.1525x; 1.0192x over previous
import numpy as np
import ml_dtypes

import concourse.bass as bass
import concourse.mybir as mybir
from concourse.bass_utils import run_bass_kernel_spmd

F32 = mybir.dt.float32
BF16 = mybir.dt.bfloat16
AF = mybir.ActivationFunctionType
ALU = mybir.AluOpType
AX = mybir.AxisListType
BF = ml_dtypes.bfloat16

C = 1024
NH = 8
DH = 128
FF = 4096
MASKV = 1280.0
SCALE = 1.0 / 32.0
THS = (0.24419428096993973, 0.4639691338428854, 0.6861859295255306,
       0.9059607823984763, 1.0839784132255623)

ENGS = ("pe", "ve", "ae", "sp", "gp")


def _build_masks():
    mA = np.full((128, 132), -MASKV, np.float32)
    mB = np.full((128, 132), -MASKV, np.float32)
    for p in range(128):
        blk = 4 * (p // 4)
        mA[p, blk + 2: blk + 6] = 0.0
        jB = blk if p % 4 < 2 else blk + 4
        mB[p, jB: jB + 4] = 0.0
    return mA.astype(BF), mB.astype(BF)


class Walk:

    def __init__(self):
        self.prog = {e: [] for e in ENGS}
        self.cnt = {}
        self.wmarks = {}
        self.rmarks = {}
        self.pending = {e: [] for e in ENGS}
        self.barrier_waits = {}
        self.stream_eng = {}

    def barrier(self, skip_streams=()):
        vals = {}
        for e in ENGS:
            if self.prog[e]:
                vals[e] = self._mark_val(e)
        for s, eng in self.stream_eng.items():
            if s not in skip_streams:
                vals[s] = self.cnt.get(s, 0)
        for e in ENGS:
            self.barrier_waits[e] = dict(vals)

    def _mark_val(self, eng):
        if self.pending[eng] or (self.prog[eng] and self.prog[eng][-1]["inc"] is None):
            return self._flush(eng)
        return self.cnt.get(eng, 0)


class Walk2(Walk):
    def op(self, eng, fn, reads=(), writes=(), dma=None, ckpt=False, tag=""):
        waits = {}
        for b in reads:
            for e2 in ENGS:
                if b in self.pending[e2]:
                    self._flush(e2)
            for s, v in self.wmarks.get(b, {}).items():
                waits[s] = max(waits.get(s, 0), v)
        for b in writes:
            for e2 in ENGS:
                if b in self.pending[e2] and e2 != eng:
                    self._flush(e2)
            for s, v in self.wmarks.get(b, {}).items():
                if s != eng:
                    waits[s] = max(waits.get(s, 0), v)
            rm = self.rmarks.get(b, {})
            for s in list(rm):
                v = rm[s]
                if v is None:
                    v = self._flush(s)
                    rm[s] = v
                if s != eng:
                    waits[s] = max(waits.get(s, 0), v)
        if dma is not None:
            self.stream_eng[dma] = eng
            if ckpt:
                prev = self.cnt.get(dma, 0)
                if prev:
                    waits[dma] = max(waits.get(dma, 0), prev)
            v = self.cnt.get(dma, 0) + 16
            self.cnt[dma] = v
            inc = (dma, 16)
            for b in writes:
                self.wmarks[b] = {dma: v}
                self.rmarks[b] = {}
            for b in reads:
                self.rmarks.setdefault(b, {})[dma] = v
        else:
            inc = None
            for b in writes:
                self.pending[eng].append(b)
                self.wmarks[b] = {}
                self.rmarks[b] = {}
            for b in reads:
                self.rmarks.setdefault(b, {})[eng] = None
        bw = self.barrier_waits.pop(eng, None)
        if bw:
            for s, v in bw.items():
                waits[s] = max(waits.get(s, 0), v)
        self.prog[eng].append(dict(fn=fn, waits=waits, inc=inc, tag=tag))
        return len(self.prog[eng]) - 1

    def _flush(self, eng):
        ops = self.prog[eng]
        if not ops:
            return self.cnt.get(eng, 0)
        rec = ops[-1]
        if rec["inc"] is None and rec.get("_dma", False) is False:
            v = self.cnt.get(eng, 0) + 1
            self.cnt[eng] = v
            rec["inc"] = (eng, 1)
            for b in self.pending[eng]:
                self.wmarks.setdefault(b, {})[eng] = v
            self.pending[eng] = []
            for b, rm in self.rmarks.items():
                if rm.get(eng, 0) is None:
                    rm[eng] = v
            return v
        if self.pending[eng] or any(rm.get(eng, 0) is None
                                    for rm in self.rmarks.values()):
            if rec["inc"] is not None and rec["inc"][0] != eng:
                extra = rec.get("inc2")
                if extra is None:
                    v = self.cnt.get(eng, 0) + 1
                    self.cnt[eng] = v
                    rec["inc2"] = (eng, 1)
                for b in self.pending[eng]:
                    self.wmarks.setdefault(b, {})[eng] = self.cnt[eng]
                self.pending[eng] = []
                for b, rm in self.rmarks.items():
                    if rm.get(eng, 0) is None:
                        rm[eng] = self.cnt[eng]
                return self.cnt[eng]
        return self.cnt.get(eng, 0)


def build_program(L=4096):
    NT = L // 128
    NS = L // 512
    nc = bass.Bass("TRN2", target_bir_lowering=False)

    x_d = nc.dram_tensor("x", [L, C], F32, kind="ExternalInput").ap()
    wqkv_d = nc.dram_tensor("w_qkv", [C, 3 * C], F32, kind="ExternalInput").ap()
    wout_d = nc.dram_tensor("w_out", [C, C], F32, kind="ExternalInput").ap()
    bout_d = nc.dram_tensor("b_out", [1, C], F32, kind="ExternalInput").ap()
    wfcT_d = nc.dram_tensor("w_fcT", [C, FF], F32, kind="ExternalInput").ap()
    wprojT_d = nc.dram_tensor("w_projT", [FF, C], F32, kind="ExternalInput").ap()
    out_d = nc.dram_tensor("out", [L, C], F32, kind="ExternalOutput").ap()

    xnT_sc = nc.dram_tensor("xnT_sc", [C, L], BF16, kind="Internal").ap()
    st_sc = nc.dram_tensor("st_sc", [L, C], BF16, kind="Internal").ap()
    wfcB_d = nc.dram_tensor("wfc_bf", [C, FF], BF16, kind="Internal").ap()
    wprB_d = nc.dram_tensor("wpr_bf", [FF, C], BF16, kind="Internal").ap()

    mA_np, mB_np = _build_masks()
    eye_h = nc.inline_tensor(np.eye(128, dtype=BF), "c_eye")
    ones_h = nc.inline_tensor(np.ones((1, 128), np.float32), "c_ones")
    mA_h = nc.inline_tensor(mA_np, "c_mA")
    mB_h = nc.inline_tensor(mB_np, "c_mB")
    rz0 = np.ones((128, 1), np.float32); rz0[0:2] = 0.0
    rzL = np.ones((128, 1), np.float32); rzL[126:128] = 0.0
    rf0 = np.zeros((128, 1), np.float32); rf0[0:2] = -0.5
    rfL = np.zeros((128, 1), np.float32); rfL[126:128] = -0.5
    zer_h = nc.inline_tensor(np.zeros((128, 1), np.float32), "c_zero")
    eps_h = nc.inline_tensor(np.full((128, 1), 1e-5, np.float32), "c_eps")
    z16_h = nc.inline_tensor(np.zeros((128, 16), BF), "c_z16")
    rz0_h = nc.inline_tensor(rz0, "c_rz0")
    rzL_h = nc.inline_tensor(rzL, "c_rzL")
    rf0_h = nc.inline_tensor(rf0, "c_rf0")
    rfL_h = nc.inline_tensor(rfL, "c_rfL")

    A = {}
    SB_BASE = 16448
    def sb(name, shape, dtype, off):
        A[name] = nc.alloc_sbuf_tensor_at(name, shape, dtype, offset=SB_BASE + off)
        return A[name]

    sb("z16", [128, 16], BF16, -64)
    sb("eye", [128, 128], BF16, 0)
    sb("ones1", [1, 128], F32, 256)
    sb("mA", [128, 132], BF16, 768)
    sb("mB", [128, 132], BF16, 1056)
    sb("rz0", [128, 1], F32, 1344)
    sb("rzL", [128, 1], F32, 1376)
    sb("rf0", [128, 1], F32, 1408)
    sb("rfL", [128, 1], F32, 1440)
    sb("zero", [128, 1], F32, 1472)
    sb("eps", [128, 1], F32, 1504)
    sb("bout", [1, C], F32, 1536)
    sb("b2", [128, C], F32, 5632)
    nc.const_aps.register(F32, 0.0, A["zero"].ap())
    WQ_OFF = 9728
    sb("wq", [128, 8 * 3 * C], BF16, WQ_OFF)
    sb("wo", [128, 8 * C], BF16, WQ_OFF + 49152)
    sb("wfcT", [128, 8 * FF], BF16, WQ_OFF)
    XNT_OFF = WQ_OFF + 65536
    sb("xnT", [128, 8 * L], BF16, XNT_OFF)
    sb("wprT", [128, 32 * C], BF16, XNT_OFF)
    AR = XNT_OFF + 16 * L
    sb("qr", [128, 2 * NH * 512], BF16, AR)
    sb("kr", [128, 2 * NH * 512], BF16, AR + 16384)
    sb("vr", [128, 5 * C], BF16, AR + 32768)
    sb("xt", [128, 2 * C], F32, AR + 43008)
    sb("vx32", [2, C], BF16, AR + 43008)
    sb("kr8", [128, 16], BF16, AR + 45056)
    sb("xnb", [128, 2 * C], BF16, AR + 51200)
    sb("expA", [128, 528], BF16, AR + 55296)
    sb("expB", [128, 528], BF16, AR + 56352)
    sb("wb", [128, 4 * 528], BF16, AR + 57408)
    sb("stt", [128, 2 * C], BF16, AR + 61632)
    sb("wtm", [128, 1024], BF16, AR + 65728)
    sb("osb", [128, 1024], BF16, AR + 67776)
    sb("wte", [4, 1024], BF16, AR + 69824)
    sb("astat", [128, 16], F32, AR + 71872)
    sb("rsb", [128, 32], F32, AR + 71936)
    assert SB_BASE + AR + 72064 <= 229368, f"sbuf overflow B: {SB_BASE + AR + 72064}"
    sb("xnTc", [128, 4096], BF16, AR)
    sb("hsb", [128, 8192], BF16, AR + 8192)
    sb("hrl", [128, 512], BF16, AR + 24576)
    sb("acc", [128, 2048], F32, AR + 25600)
    sb("gsb", [128, 2048], BF16, AR + 33792)
    sb("sti", [128, 4096], BF16, AR + 37888)
    sb("xi", [128, 4096], F32, AR + 46080)
    sb("ot", [128, 2048], F32, AR + 62464)
    assert SB_BASE + AR + 70656 <= 229376

    pp0 = nc.alloc_psum_tensor("pp0", [128, 512], F32)
    pp1 = nc.alloc_psum_tensor("pp1", [128, 512], F32)
    pS0 = nc.alloc_psum_tensor("pS0", [128, 512], F32)
    pS1 = nc.alloc_psum_tensor("pS1", [128, 512], F32)
    pS2 = nc.alloc_psum_tensor("pS2", [128, 512], F32)
    pT = nc.alloc_psum_tensor("pT", [128, 1024], BF16)
    pO = nc.alloc_psum_tensor("pO", [128, 1024], F32)

    w = Walk2()
    rotc = [0]

    def rot():
        k = rotc[0] & 1
        rotc[0] += 1
        return (pp0, "pp0") if k == 0 else (pp1, "pp1")

    def dma(eng, dst, src, stream, reads, writes, ckpt=False, transpose=False):
        if transpose:
            w.op(eng, lambda E, d=dst, s=src: E.dma_start_transpose(d, s),
                 reads=reads, writes=writes, dma=stream, ckpt=ckpt)
        else:
            w.op(eng, lambda E, d=dst, s=src: E.dma_start(d, s),
                 reads=reads, writes=writes, dma=stream, ckpt=ckpt)

    def flags_for(recs):
        first = {}
        last = {}
        for i, (bank, _fn, _r, _w) in enumerate(recs):
            if bank not in first:
                first[bank] = i
            last[bank] = i
        for i, (bank, fn, rd, wr) in enumerate(recs):
            st = (i == first[bank])
            sp_ = (i == last[bank])
            w.op("pe", (lambda E, f=fn, a=st, b=sp_: f(E, a, b)),
                 reads=rd, writes=wr)

    for name, hdl in (("eye", eye_h), ("ones1", ones_h), ("mA", mA_h),
                      ("mB", mB_h), ("rz0", rz0_h), ("rzL", rzL_h),
                      ("rf0", rf0_h), ("rfL", rfL_h), ("zero", zer_h),
                      ("eps", eps_h), ("z16", z16_h)):
        dma("sp", A[name][:], hdl.ap(), "smc", (), ("consts", "bout"))
    dma("sp", A["bout"][:], bout_d, "smc", (), ("consts", "bout"))
    for _e in ENGS:
        w.barrier_waits[_e] = {"smc": w.cnt["smc"]}
    for kc in range(8):
        dma("gp", A["wq"][:, 3 * C * kc: 3 * C * (kc + 1)],
            wqkv_d[128 * kc:128 * (kc + 1), :], "swq", (), ("wq",))
    for kc in range(8):
        dma("gp", A["wo"][:, C * kc: C * (kc + 1)],
            wout_d[128 * kc:128 * (kc + 1), :], "swo", (), ("wo",))
    for q in range(4):
        dma("gp", wfcB_d[256 * q: 256 * (q + 1), :],
            wfcT_d[256 * q: 256 * (q + 1), :], "swc", (), ("wfcbf",))
    for q in range(4):
        dma("gp", wprB_d[1024 * q: 1024 * (q + 1), :],
            wprojT_d[1024 * q: 1024 * (q + 1), :], "swc", (), ("wprbf",))
    w.op("pe", lambda E: E.matmul(pp0[:], A["ones1"][:], A["bout"][:, 0:512],
                                  start=True, stop=True),
         reads=("bout", "consts"), writes=("pp0",))
    w.op("pe", lambda E: E.matmul(pp1[:], A["ones1"][:], A["bout"][:, 512:1024],
                                  start=True, stop=True),
         reads=("bout", "consts"), writes=("pp1",))
    w.op("ae", lambda E: E.activation(A["b2"][:, 0:512], pp0[:], AF.Copy, scale=2.0),
         reads=("pp0",), writes=("b2",))
    w.op("ae", lambda E: E.activation(A["b2"][:, 512:1024], pp1[:], AF.Copy, scale=2.0),
         reads=("pp1",), writes=("b2",))
    w.op("ve", lambda E: E.tensor_copy(
        A["kr"][:, 0: NH * 512].rearrange("p (h c) -> p h c", h=NH)[:, :, 0:2],
        A["z16"][:].rearrange("p (h c) -> p h c", h=NH)),
         reads=("consts",), writes=("k0",))

    inv_c = 1.0 / C

    def ln_load(t):
        p = t % 2
        dma("sp", A["xt"][:, C * p: C * (p + 1)], x_d[128 * t:128 * (t + 1), :],
            "sx" + str(p), (), (f"xt{p}",), ckpt=True)

    def ln_stats(t):
        p = t % 2
        xt = A["xt"][:, C * p: C * (p + 1)]
        xn = A["xnb"][:, C * p: C * (p + 1)]
        st = A["astat"][:, 8 * p: 8 * p + 8]
        w.op("ve", lambda E, o=st, i=xt: E.tensor_reduce(o[:, 0:1], i, AX.X, ALU.add),
             reads=(f"xt{p}",), writes=(f"astat{p}",))
        w.op("ae", lambda E, i=xt, s=st, d=xn: E.activation(d, i, AF.Square,
                                                            accum_out=s[:, 1:2]),
             reads=(f"xt{p}",), writes=(f"astat{p}", f"xnb{p}"))
        w.op("ve", lambda E, s=st: E.tensor_scalar(s[:, 2:3], s[:, 0:1], inv_c, None, ALU.mult),
             reads=(f"astat{p}",), writes=(f"astat{p}",))
        w.op("ve", lambda E, s=st: E.tensor_tensor(s[:, 3:4], s[:, 2:3], s[:, 2:3], ALU.mult),
             reads=(f"astat{p}",), writes=(f"astat{p}",))
        w.op("ve", lambda E, s=st: E.tensor_scalar(s[:, 3:4], s[:, 3:4], -1e-5, None, ALU.add),
             reads=(f"astat{p}",), writes=(f"astat{p}",))
        w.op("ve", lambda E, s=st: E.scalar_tensor_tensor(
            s[:, 4:5], s[:, 1:2], inv_c, s[:, 3:4], ALU.mult, ALU.subtract),
             reads=(f"astat{p}",), writes=(f"astat{p}",))
        w.op("ae", lambda E, s=st: E.activation(s[:, 5:6], s[:, 4:5], AF.Ln),
             reads=(f"astat{p}",), writes=(f"astat{p}",))
        w.op("ae", lambda E, s=st: E.activation(s[:, 6:7], s[:, 5:6], AF.Exp,
                                                scale=-0.5),
             reads=(f"astat{p}",), writes=(f"astat{p}",))
        w.op("ve", lambda E, s=st: E.scalar_tensor_tensor(
            s[:, 7:8], s[:, 2:3], -1.0, s[:, 6:7], ALU.mult, ALU.mult),
             reads=(f"astat{p}",), writes=(f"astat{p}",))

    def ln_apply(t):
        p = t % 2
        xt = A["xt"][:, C * p: C * (p + 1)]
        xn = A["xnb"][:, C * p: C * (p + 1)]
        st = A["astat"][:, 8 * p: 8 * p + 8]
        w.op("ae", lambda E, i=xt, s=st, d=xn: E.activation(
            d, i, AF.Identity, scale=s[:, 6:7], bias=s[:, 7:8]),
             reads=(f"xt{p}", f"astat{p}"), writes=(f"xnb{p}",))

    def xn_transpose(t):
        p4 = t % 2
        bkey = "pT"
        bb = pT[:]
        for c in range(8):
            w.op("pe", lambda E, d=bb[:, 128 * c: 128 * (c + 1)],
                 i=A["xnb"][:, C * p4 + 128 * c: C * p4 + 128 * (c + 1)]:
                 E.transpose(d, i, A["eye"][:]),
                 reads=(f"xnb{p4}", "consts"), writes=(bkey,))
        dst = A["xnT"][:].rearrange("p (c l) -> p c l", c=8)[
            :, :, 128 * t: 128 * (t + 1)]
        src = bb.rearrange("p (c l) -> p c l", c=8)
        if t % 2 == 0:
            w.op("ve", lambda E, d=dst, s=src: E.tensor_copy(d, s),
                 reads=(bkey,), writes=(f"xnTs{t // 4}", "xnT"))
        else:
            w.op("ae", lambda E, d=dst, s=src: E.activation(d, s, AF.Copy),
                 reads=(bkey,), writes=(f"xnTs{t // 4}", "xnT"))

    def qk_super(s, part):
        isq = part < 2
        hs = 4 * (part % 2)
        sl = s % 2
        if isq:
            tok0, N, doff = 512 * s, 512, 0
        elif s > 0:
            tok0, N, doff = 512 * s - 2, 512, 0
        else:
            tok0, N, doff = 0, 510, 2
        qkoff = 0 if isq else C
        dst_sb = A["qr"] if isq else A["kr"]
        key = (f"q{sl}" if isq else f"k{sl}")
        rds = ("wq", f"xnTs{s}") + ((f"xnTs{s - 1}",) if (not isq and s > 0) else ())
        for h in range(hs, hs + 4):
            bank, bkey = rot()
            for kc in range(8):
                w.op("pe", (lambda E, d=bank[:, doff: doff + N],
                            lt=A["wq"][:, 3 * C * kc + qkoff + 128 * h:
                                       3 * C * kc + qkoff + 128 * (h + 1)],
                            r=A["xnT"][:, L * kc + tok0: L * kc + tok0 + N],
                            st_=(kc == 0), sp_=(kc == 7):
                            E.matmul(d, lt, r, start=st_, stop=sp_)),
                     reads=rds, writes=(bkey,))
            w.op("ae", lambda E, d=dst_sb[:, 4096 * sl + 512 * h + doff:
                                          4096 * sl + 512 * h + doff + N],
                 s_=bank[:, doff: doff + N]: E.activation(d, s_, AF.Copy),
                 reads=(bkey,), writes=(key,))

    def v_tile(u):
        vstart = 0 if u == 0 else 128 * u - 2
        sup = vstart // 512
        rds = ("wq", f"xnTs{u // 4}") + ((f"xnTs{sup}",) if sup != u // 4 else ())
        for nh in range(2):
            bank, bkey = rot()
            for kc in range(8):
                w.op("pe", (lambda E, d=bank[:],
                            lt=A["xnT"][:, L * kc + vstart: L * kc + vstart + 128],
                            r=A["wq"][:, 3 * C * kc + 2 * C + 512 * nh:
                                      3 * C * kc + 2 * C + 512 * (nh + 1)],
                            st_=(kc == 0), sp_=(kc == 7):
                            E.matmul(d, lt, r, start=st_, stop=sp_)),
                     reads=rds, writes=(bkey,))
            w.op("ae", lambda E, d=A["vr"][:, C * (u % 5) + 512 * nh:
                                           C * (u % 5) + 512 * (nh + 1)],
                 s_=bank[:]: E.activation(d, s_, AF.Copy),
                 reads=(bkey,), writes=(f"v{u % 5}",))

    def kr8_prod():
        bank, bkey = rot()
        recs = []
        for h in range(8):
            for kc in range(8):
                recs.append((bkey, (lambda E, st_, sp_,
                                    d=bank[:, 2 * h: 2 * h + 2],
                                    lt=A["wq"][:, 3 * C * kc + C + 128 * h:
                                               3 * C * kc + C + 128 * (h + 1)],
                                    r=A["xnT"][:, L * kc + L - 2: L * kc + L]:
                                    E.matmul(d, lt, r, start=st_, stop=sp_)),
                             ("wq", f"xnTs{NS - 1}"), (bkey,)))
        flags_for(recs)
        w.op("ve", lambda E, d=A["kr8"][:], s_=bank[:, 0:16]: E.tensor_copy(d, s_),
             reads=(bkey,), writes=("kr8",))

    def vx32_prod():
        for nh in range(2):
            bank, bkey = rot()
            for kc in range(8):
                w.op("pe", (lambda E, d=bank[0:2, :],
                            lt=A["xnT"][:, L * kc + L - 2: L * kc + L],
                            r=A["wq"][:, 3 * C * kc + 2 * C + 512 * nh:
                                      3 * C * kc + 2 * C + 512 * (nh + 1)],
                            st_=(kc == 0), sp_=(kc == 7):
                            E.matmul(d, lt, r, start=st_, stop=sp_)),
                     reads=("wq", f"xnTs{NS - 1}"), writes=(bkey,))
            w.op("ve", lambda E, d=A["vx32"][0:2, 512 * nh: 512 * (nh + 1)],
                 s_=bank[0:2, :]: E.tensor_copy(d, s_),
                 reads=(bkey,), writes=("vx32",))

    SBANKS = ((pS0, "bkS0"), (pS1, "bkS1"), (pS2, "bkS2"))

    def scores(t, g):
        s_sup, j = t // 4, t % 4
        ksl = (s_sup % 2)
        recs = []
        for hh in range(4):
            h = 4 * g + hh
            lhsT = A["qr"][:, 4096 * ksl + 512 * h + 128 * j:
                           4096 * ksl + 512 * h + 128 * j + 128]
            for seti, mkey in ((0, "mA"), (1, "mB")):
                r = 4 * seti + hh
                ps, bk = SBANKS[r // 3]
                reg0 = 132 * (r % 3)
                segs = []
                if j < 3:
                    segs.append((A["kr"][:, 4096 * ksl + 512 * h + 128 * j:
                                         4096 * ksl + 512 * h + 128 * j + 132],
                                 0, 132, f"k{ksl}"))
                else:
                    segs.append((A["kr"][:, 4096 * ksl + 512 * h + 384:
                                         4096 * ksl + 512 * h + 512],
                                 0, 128, f"k{ksl}"))
                    if t < NT - 1:
                        segs.append((A["kr"][:, 4096 * ((s_sup + 1) % 2) + 512 * h:
                                             4096 * ((s_sup + 1) % 2) + 512 * h + 4],
                                     128, 132, f"k{(s_sup + 1) % 2}"))
                    else:
                        segs.append((A["kr8"][:, 2 * h: 2 * h + 2],
                                     128, 130, "kr8"))
                recs.append((bk, (lambda E, st_, sp_, d=ps[:, reg0: reg0 + 132],
                                  r=A[mkey][:]:
                                  E.matmul(d, A["eye"][:], r, start=st_, stop=sp_)),
                             ("consts",), (bk,)))
                for rhs, c0, c1, kkey in segs:
                    dst = ps[:, reg0 + c0: reg0 + c1]
                    recs.append((bk, (lambda E, st_, sp_, d=dst, lt=lhsT, r=rhs:
                                      E.matmul(d, lt, r, start=st_, stop=sp_)),
                                 (f"q{ksl}", kkey), (bk,)))
        flags_for(recs)

    def softmax_merge(t, g):
        s = g
        ss = str(s)
        s2 = 2 * (t % 2) + g
        expA = A["expA"][:, 0:528]
        expB = A["expB"][:, 0:528]
        rs = A["rsb"]
        wkey = f"wb{t % 2}{g}"
        w.op("ae", lambda E, o=expA[:, 0:396], p_=pS0:
             E.activation(o, p_[:, 0:396], AF.Exp, scale=SCALE),
             reads=("bkS0",), writes=("expA",))
        w.op("ae", lambda E, o=expA[:, 396:528], p_=pS1:
             E.activation(o, p_[:, 0:132], AF.Exp, scale=SCALE),
             reads=("bkS1",), writes=("expA",))
        w.op("ae", lambda E, o=expB[:, 0:264], p_=pS1:
             E.activation(o, p_[:, 132:396], AF.Exp, scale=SCALE),
             reads=("bkS1",), writes=("expB",))
        w.op("ae", lambda E, o=expB[:, 264:528], p_=pS2:
             E.activation(o, p_[:, 0:264], AF.Exp, scale=SCALE),
             reads=("bkS2",), writes=("expB",))
        w.op("ve", lambda E, i=expA, o=rs[:, 16 * s: 16 * s + 4]:
             E.tensor_reduce(o, i.rearrange("p (h j) -> p h j", h=4), AX.X, ALU.add),
             reads=("expA",), writes=("rs" + ss,))
        w.op("ve", lambda E, i=expB, o=rs[:, 16 * s + 4: 16 * s + 8]:
             E.tensor_reduce(o, i.rearrange("p (h j) -> p h j", h=4), AX.X, ALU.add),
             reads=("expB",), writes=("rs" + ss,))
        w.op("ve", lambda E, o=rs[:, 16 * s + 8: 16 * s + 16],
             i=rs[:, 16 * s: 16 * s + 8]: E.reciprocal(o, i),
             reads=("rs" + ss,), writes=("rs" + ss,))
        if t == 0 or t == NT - 1:
            rzk = "rz0" if t == 0 else "rzL"
            w.op("ve", lambda E, o=rs[:, 16 * s + 12: 16 * s + 16], z=A[rzk][:, 0:1]:
                 E.tensor_scalar(o, o, z, None, ALU.mult),
                 reads=("rs" + ss, "consts"), writes=("rs" + ss,))
        for hh in range(4):
            wdst = A["wb"][:, 528 * s2 + 132 * hh: 528 * s2 + 132 * (hh + 1)]
            w.op("ve", lambda E, o=wdst, i=expA[:, 132 * hh: 132 * (hh + 1)],
                 r=rs[:, 16 * s + 8 + hh: 16 * s + 9 + hh]:
                 E.tensor_scalar(o, i, r, None, ALU.mult),
                 reads=("expA", "rs" + ss), writes=(wkey,))
            w.op("ve", lambda E, o=wdst, i=expB[:, 132 * hh: 132 * (hh + 1)],
                 r=rs[:, 16 * s + 12 + hh: 16 * s + 13 + hh]:
                 E.scalar_tensor_tensor(o, i, r, o, ALU.mult, ALU.add),
                 reads=("expB", "rs" + ss, wkey), writes=(wkey,))

    def wt_and_o(t, g):
        s = g
        s2 = 2 * (t % 2) + g
        ss = f"wb{t % 2}{g}"
        shift = 2 if t == 0 else 0
        bkey = "pT"
        bb = pT[:]
        for hh in range(4):
            w.op("pe", lambda E, d=bb[:, 128 * hh: 128 * (hh + 1)],
                 i=A["wb"][:, 528 * s2 + 132 * hh + shift: 528 * s2 + 132 * hh + shift + 128]:
                 E.transpose(d, i, A["eye"][:]),
                 reads=(ss, "consts"), writes=(bkey,))
        for hh in range(4):
            w.op("pe", lambda E, d=bb[0:4, 512 + 128 * hh: 512 + 128 * (hh + 1)],
                 i=A["wb"][:, 528 * s2 + 132 * hh + 128: 528 * s2 + 132 * (hh + 1)]:
                 E.transpose(d, i, A["eye"][:]),
                 reads=(ss, "consts"), writes=(bkey,))
        w.op("ve", lambda E, d=A["wtm"][:, 512 * g: 512 * (g + 1)], s_=bb[:, 0:512]:
             E.tensor_copy(d, s_),
             reads=(bkey,), writes=("wtm",))
        w.op("ve", lambda E, d=A["wte"][0:4, 512 * g: 512 * (g + 1)],
             s_=bb[0:4, 512:1024]: E.tensor_copy(d, s_),
             reads=(bkey,), writes=("wte",))
        if t == 0:
            w.op("ve", lambda E, d=A["wte"][0:2, 512 * g: 512 * (g + 1)],
                 i=A["wtm"][0:2, 512 * g: 512 * (g + 1)]:
                 E.tensor_scalar(d, i, 0.0, None, ALU.mult),
                 reads=("wte", "wtm"), writes=("wte",))
        obk = "bkO" + str(g)
        orecs = []
        for hh in range(4):
            h = 4 * g + hh
            dst = pO[:, 128 * h: 128 * (h + 1)]
            orecs.append((obk, (lambda E, st_, sp_, d=dst,
                                lt=A["vr"][:, C * (t % 5) + 128 * h:
                                           C * (t % 5) + 128 * (h + 1)],
                                r=A["wtm"][:, 128 * h: 128 * (h + 1)]:
                                E.matmul(d, lt, r, start=st_, stop=sp_)),
                          (f"v{t % 5}", "wtm"), (obk,)))
            if t == 0:
                lt2 = A["vr"][0:4, C * 1 + 128 * h: C * 1 + 128 * h + 128]
                r2 = A["wte"][0:4, 128 * h: 128 * (h + 1)]
                vkey = "v1"
            elif t == NT - 1:
                lt2 = A["vx32"][0:2, 128 * h: 128 * h + 128]
                r2 = A["wte"][0:2, 128 * h: 128 * (h + 1)]
                vkey = "vx32"
            else:
                lt2 = A["vr"][0:4, C * ((t + 1) % 5) + 128 * h:
                              C * ((t + 1) % 5) + 128 * h + 128]
                r2 = A["wte"][0:4, 128 * h: 128 * (h + 1)]
                vkey = f"v{(t + 1) % 5}"
            orecs.append((obk, (lambda E, st_, sp_, d=dst, lt=lt2, r=r2:
                                E.matmul(d, lt, r, start=st_, stop=sp_)),
                          (vkey, "wte"), (obk,)))
        flags_for(orecs)
        w.op("ve", lambda E, d=A["osb"][:, 512 * g: 512 * (g + 1)],
             s_=pO[:, 512 * g: 512 * (g + 1)]: E.tensor_copy(d, s_),
             reads=(obk,), writes=("osb",))

    def st_tile(t):
        recs = []
        for nh in range(2):
            bk = "bkO" + str(nh)
            dstp = pO[:, 512 * nh: 512 * (nh + 1)]
            for cin in range(8):
                recs.append((bk, (lambda E, st_, sp_, d=dstp,
                                  lt=A["osb"][:, 128 * cin: 128 * (cin + 1)],
                                  r=A["wo"][:, C * cin + 512 * nh:
                                            C * cin + 512 * (nh + 1)]:
                                  E.matmul(d, lt, r, start=st_, stop=sp_)),
                             ("osb", "wo"), (bk,)))
        flags_for(recs)
        p = t % 2
        ks = str(p)
        dst = A["stt"][:, C * p: C * (p + 1)]
        w.op("ve", lambda E, d=dst[:, 0:512]: E.tensor_tensor(d, pO[:, 0:512], A["b2"][:, 0:512], ALU.add),
             reads=("bkO0", "b2"), writes=("stt" + ks,))
        w.op("ve", lambda E, d=dst[:, 512:1024]: E.tensor_tensor(d, pO[:, 512:1024], A["b2"][:, 512:1024], ALU.add),
             reads=("bkO1", "b2"), writes=("stt" + ks,))
        if t == 0 or t == NT - 1:
            rfk = "rf0" if t == 0 else "rfL"
            w.op("ve", lambda E, d=dst, z=A[rfk][:, 0:1]:
                 E.scalar_tensor_tensor(d, A["b2"][:], z, d, ALU.mult, ALU.add),
                 reads=("b2", "consts", "stt" + ks), writes=("stt" + ks,))
        dma("sp", st_sc[128 * t:128 * (t + 1), :], dst, "sst" + ks,
            ("stt" + ks,), ("st_sc",), ckpt=True)

    def wprT_batch(b):
        wkeys = (f"xnTs{2 * b}", f"xnTs{2 * b + 1}", "xnT", f"wprT{b}")
        for jj in range(8):
            fc = 4 * jj + b
            dma("gp", A["wprT"][:, C * fc: C * (fc + 1)],
                wprB_d[128 * fc:128 * (fc + 1), :],
                f"swp{b}", ("wprbf",), wkeys)

    def xnsc_block(b):
        keys = (f"xnTs{2 * b}", f"xnTs{2 * b + 1}", "xnT")
        for c in range(8):
            dma("sp", xnT_sc[128 * c:128 * (c + 1), 1024 * b: 1024 * (b + 1)],
                A["xnT"][:, L * c + 1024 * b: L * c + 1024 * (b + 1)],
                "sxts", keys, ("xnT_sc",))

    assert NT % 4 == 0 and NS >= 2
    for j in range(2):
        ln_load(j)
    ln_stats(0)
    ln_stats(1)
    for i in range(NS + 2):
        p, qcons = i - 1, i - 2
        if i == NS:
            kr8_prod()
            vx32_prod()
        if i == NS + 1:
            wprT_batch(3)
        chunks = ([("qk", p, 0), ("qk", p, 1), ("qk", p, 2), ("qk", p, 3),
                   ("v", 4 * p), ("v", 4 * p + 1), ("v", 4 * p + 2), ("v", 4 * p + 3)]
                  if 0 <= p < NS else [])
        for j in range(4):
            tln = 4 * i + j
            t = 4 * qcons + j
            if i < NS:
                ln_apply(tln)
                if tln + 2 < NT:
                    ln_load(tln + 2)
            if qcons >= 0:
                scores(t, 0)
                softmax_merge(t, 0)
            for ch in chunks[2 * j: 2 * j + 1]:
                (qk_super(*ch[1:]) if ch[0] == "qk" else v_tile(ch[1]))
            if qcons >= 0:
                scores(t, 1)
                if t >= 1:
                    wt_and_o(t - 1, 0)
                    wt_and_o(t - 1, 1)
                softmax_merge(t, 1)
            for ch in chunks[2 * j + 1: 2 * j + 2]:
                (qk_super(*ch[1:]) if ch[0] == "qk" else v_tile(ch[1]))
            if qcons >= 0 and t >= 1:
                st_tile(t - 1)
            if i < NS:
                xn_transpose(tln)
                if tln + 2 < NT:
                    ln_stats(tln + 2)
        if i >= 2 and (i - 2) % 2 == 0 and (i - 2) // 2 < 3:
            xnsc_block((i - 2) // 2)
        if i == NS:
            xnsc_block(3)
        if i in (3, 5, 7):
            wprT_batch((i - 3) // 2)
        if i == NS:
            for c in range(6):
                dma("gp", A["wfcT"][:, FF * c: FF * (c + 1)],
                    wfcB_d[128 * c:128 * (c + 1), :],
                    "swt", ("wfcbf",), ("wfcT6", "wq"))
    wt_and_o(NT - 1, 0)
    wt_and_o(NT - 1, 1)
    st_tile(NT - 1)

    for c in range(6, 8):
        dma("gp", A["wfcT"][:, FF * c: FF * (c + 1)], wfcB_d[128 * c:128 * (c + 1), :],
            "swt3", ("wfcbf",), ("wfcT2", "wo"))
    w.barrier(skip_streams=("swt", "swt3", "swp0", "swp1", "swp2", "swp3"))

    NU = L // 256
    hbanks = ((pp0[:, 0:256], "pp0"), (pO[:, 0:256], "bkO0"), (pO[:, 512:768], "bkO1"))

    def c_loads(u):
        up = u % 2
        us = str(up)
        cink = "cin" + us
        for kc in range(8):
            dma("sp", A["xnTc"][:, 2048 * up + 256 * kc: 2048 * up + 256 * (kc + 1)],
                xnT_sc[128 * kc:128 * (kc + 1), 256 * u: 256 * (u + 1)],
                "sc" + us, ("xnT_sc",), (cink,), ckpt=(kc == 0))
        for ii in range(2):
            tt = 2 * u + ii
            dma("sp", A["sti"][:, 2048 * up + C * ii: 2048 * up + C * (ii + 1)],
                st_sc[128 * tt:128 * (tt + 1), :], "sc" + us,
                ("st_sc",), (cink,))
            dma("sp", A["xi"][:, 2048 * up + C * ii: 2048 * up + C * (ii + 1)],
                x_d[128 * tt:128 * (tt + 1), :], "sc" + us,
                (), (cink,))

    def c_hchunk(u, fc):
        up = u % 2
        us = str(up)
        dst, bk = hbanks[fc % 3]
        for kc in range(8):
            wkey = "wfcT6" if kc < 6 else "wfcT2"
            w.op("pe", (lambda E, d=dst,
                        lt=A["wfcT"][:, FF * kc + 128 * fc: FF * kc + 128 * (fc + 1)],
                        r=A["xnTc"][:, 2048 * up + 256 * kc: 2048 * up + 256 * (kc + 1)],
                        s=(kc == 0), p_=(kc == 7):
                        E.matmul(d, lt, r, start=s, stop=p_)),
                 reads=(wkey, "cin" + us), writes=(bk,))
        hr = A["hrl"][:, 256 * (fc % 2): 256 * (fc % 2 + 1)]
        w.op("ae", lambda E, d=hr, i=dst: E.activation(d, i, AF.Relu),
             reads=(bk,), writes=(f"hrl{fc % 2}",))
        w.op("ae", lambda E, d=A["hsb"][:, 256 * fc: 256 * (fc + 1)], i=hr:
             E.activation(d, i, AF.Square),
             reads=(f"hrl{fc % 2}",), writes=("hsb",))

    def c_y_snn(u, ii):
        pshalves = (pS0, pS1) if ii == 0 else (pS2, pp1)
        bks = ("bkS0", "bkS1") if ii == 0 else ("bkS2", "pp1")
        recs = []
        fcs = (sorted(range(32), key=lambda fc: (fc % 4, fc)) if u < 2
               else list(range(32)))
        for fc in fcs:
            lhsT = A["hsb"][:, 256 * fc + 128 * ii: 256 * fc + 128 * (ii + 1)]
            for nh in range(2):
                rhs = A["wprT"][:, C * fc + 512 * nh: C * fc + 512 * (nh + 1)]
                dst = pshalves[nh][:]
                recs.append((bks[nh], (lambda E, s, p_, d=dst, lt=lhsT, r=rhs:
                                       E.matmul(d, lt, r, start=s, stop=p_)),
                             ("hsb", f"wprT{fc % 4}"), (bks[nh],)))
        flags_for(recs)
        ac = A["acc"][:, C * ii: C * (ii + 1)]
        for nh in range(2):
            ah = ac[:, 512 * nh: 512 * (nh + 1)]
            w.op("ve", lambda E, o=ah, y=pshalves[nh]:
                 E.tensor_single_scalar(o, y[:], THS[0], ALU.is_gt),
                 reads=(bks[nh],), writes=(f"acc{ii}",))
            for jj in range(1, 5):
                w.op("ve", lambda E, o=ah, y=pshalves[nh], th=THS[jj]:
                     E.scalar_tensor_tensor(o, y[:], th, o, ALU.is_gt, ALU.add),
                     reads=(bks[nh], f"acc{ii}"), writes=(f"acc{ii}",))

    def c_tail(u, ii):
        up = u % 2
        us = str(up)
        tt = 2 * u + ii
        ac = A["acc"][:, C * ii: C * (ii + 1)]
        g = A["gsb"][:, C * ii: C * (ii + 1)]
        w.op("ae", lambda E, o=g, i_=ac: E.activation(o, i_, AF.Silu, scale=0.2),
             reads=(f"acc{ii}",), writes=(f"gsb{ii}",))
        w.op("ve", lambda E, o=g, a=A["sti"][:, 2048 * up + C * ii: 2048 * up + C * (ii + 1)]:
             E.tensor_tensor(o, a, o, ALU.mult),
             reads=("cin" + us, f"gsb{ii}"), writes=(f"gsb{ii}",))
        ot = A["ot"][:, C * ii: C * (ii + 1)]
        w.op("ve", lambda E, o=ot, a=A["xi"][:, 2048 * up + C * ii: 2048 * up + C * (ii + 1)], b=g:
             E.tensor_tensor(o, a, b, ALU.add),
             reads=("cin" + us, f"gsb{ii}"), writes=(f"ot{ii}",))
        dma("sp", out_d[128 * tt:128 * (tt + 1), :], ot, "so" + str(tt % 2),
            (f"ot{ii}",), ("outd",), ckpt=True)

    c_loads(0)
    for u in range(NU):
        for fc in range(32):
            c_hchunk(u, fc)
            if u > 0 and fc == 3:
                c_tail(u - 1, 0)
            if u > 0 and fc == 7:
                c_tail(u - 1, 1)
            if fc == 11 and u + 1 < NU:
                c_loads(u + 1)
        c_y_snn(u, 0)
        c_y_snn(u, 1)
    c_tail(NU - 1, 0)
    c_tail(NU - 1, 1)

    w.op("sp", lambda E: E.nop(), reads=("outd",))
    final_waits = {"so0": w.cnt.get("so0", 0), "so1": w.cnt.get("so1", 0)}
    w.prog["sp"][-1]["waits"].update(final_waits)

    sems = {}
    import contextlib
    stack = contextlib.ExitStack()
    for name in set(ENGS) | set(w.cnt.keys()) | set(w.stream_eng.keys()):
        sems[name] = stack.enter_context(nc.semaphore("sem_" + name))

    block = stack.enter_context(nc.Block())

    def replay(eng_name):
        def body(E):
            watermark = {}
            for rec in w.prog[eng_name]:
                for s, v in sorted(rec["waits"].items()):
                    if v > 0 and watermark.get(s, -1) < v:
                        E.wait_ge(sems[s], v)
                        watermark[s] = v
                instr = rec["fn"](E)
                if rec["inc"] is not None:
                    instr = instr.then_inc(sems[rec["inc"][0]], rec["inc"][1])
                if rec.get("inc2") is not None:
                    instr.then_inc(sems[rec["inc2"][0]], rec["inc2"][1])
        return body

    block.tensor(replay("pe"))
    block.vector(replay("ve"))
    block.scalar(replay("ae"))
    block.sync(replay("sp"))
    block.gpsimd(replay("gp"))
    stack.close()
    return nc


_CACHE = {}


def kernel(L0_attn_x, L0_attn_w_qkv, L0_attn_w_out, L0_attn_b_out,
           L0_ffn_W_fc, L0_ffn_W_proj):
    x = np.ascontiguousarray(np.asarray(L0_attn_x, np.float32))
    wqkv = np.ascontiguousarray(np.asarray(L0_attn_w_qkv, np.float32))
    wout = np.ascontiguousarray(np.asarray(L0_attn_w_out, np.float32))
    bout = np.ascontiguousarray(np.asarray(L0_attn_b_out, np.float32)).reshape(1, C)
    wfcT = np.ascontiguousarray(np.asarray(L0_ffn_W_fc, np.float32).T)
    wprojT = np.ascontiguousarray(np.asarray(L0_ffn_W_proj, np.float32).T)
    B, L, _ = x.shape
    if "nc" not in _CACHE:
        _CACHE["nc"] = build_program(L)
    nc = _CACHE["nc"]
    in_maps = [{"x": x[b], "w_qkv": wqkv, "w_out": wout, "b_out": bout,
                "w_fcT": wfcT, "w_projT": wprojT} for b in range(B)]
    res = run_bass_kernel_spmd(nc, in_maps, core_ids=list(range(B)))
    return np.stack([res.results[b]["out"] for b in range(B)], axis=0)
